# version 23
# baseline (speedup 1.0000x reference)
import numpy as np
import concourse.bass as bass
import concourse.mybir as mybir
from concourse.bass_utils import run_bass_kernel_spmd
from contextlib import ExitStack

F32 = mybir.dt.float32
BF16 = mybir.dt.bfloat16
OP = mybir.AluOpType
ACT = mybir.ActivationFunctionType

NCORES = 8
TOK = 1024
D = 2048
HT = 512
NLN = 4
NJ = 44
NHH = 16
NPAIR = 16
NE = 32
CK = 128
NCH = HT // CK
EPS = 1e-5


class Plan:

    def __init__(self):
        self.ops = []
        self.cnt = {}
        self.waited = {}
        self.marks = {}

    def n(self, sem):
        return self.cnt.get(sem, 0)

    def op(self, eng, emit, waits=(), inc=None, amt=1, keep_self=False):
        pruned = []
        for (s, v) in waits:
            if v <= 0:
                continue
            assert v <= self.cnt.get(s, 0), f"forward wait {s}={v}>{self.cnt.get(s, 0)}"
            if _OWNER.get(s) == eng and not keep_self:
                continue
            k = (eng, s)
            if self.waited.get(k, 0) >= v:
                continue
            self.waited[k] = v
            pruned.append((s, v))
        if inc is not None:
            self.cnt[inc] = self.cnt.get(inc, 0) + amt
        self.ops.append((eng, pruned, emit, inc, amt))
        return self.cnt.get(inc, 0) if inc else None


_OWNER = {"T": "tensor", "V": "vector", "S": "scalar", "G": "gpsimd",
          "Gid": "gpsimd"}


def build_program(stage=None):
    nc = bass.Bass("TRN2", target_bir_lowering=False, detect_race_conditions=False)

    x_h = nc.dram_tensor("x", [TOK, D], F32, kind="ExternalInput")
    out_h = nc.dram_tensor("out", [TOK, D], F32, kind="ExternalOutput")
    dbg_h = (nc.dram_tensor("dbg", [128, 65536], F32, kind="ExternalOutput")
             if stage is not None else None)
    x_d = x_h.ap()
    gwb_d = nc.dram_tensor("gate_blk", [NJ, 128, NHH * 128], F32, kind="ExternalInput").ap()
    uwb_d = nc.dram_tensor("up_blk", [NJ, 128, NHH * 128], F32, kind="ExternalInput").ap()
    dwb_d = nc.dram_tensor("down_blk", [NHH, 128, NJ * 128], F32, kind="ExternalInput").ap()
    wq_d = nc.dram_tensor("wq", [2, 64, NPAIR * 64], F32, kind="ExternalInput").ap()
    wk_d = nc.dram_tensor("wk", [2, 64, NPAIR * 64], F32, kind="ExternalInput").ap()
    wv_d = nc.dram_tensor("wv", [2, 64, NPAIR * 64], F32, kind="ExternalInput").ap()
    wo_d = nc.dram_tensor("wo", [64, 64], F32, kind="ExternalInput").ap()
    bq_d = nc.dram_tensor("bq", [128, NPAIR], F32, kind="ExternalInput").ap()
    bk_d = nc.dram_tensor("bk", [128, NE], F32, kind="ExternalInput").ap()
    bv_d = nc.dram_tensor("bv", [128, NPAIR], F32, kind="ExternalInput").ap()
    bo_d = nc.dram_tensor("bo", [128, 1], F32, kind="ExternalInput").ap()
    gs_d = nc.dram_tensor("gs", [128, NJ], F32, kind="ExternalInput").ap()
    us_d = nc.dram_tensor("us", [128, NJ], F32, kind="ExternalInput").ap()
    ds_d = nc.dram_tensor("ds", [128, NHH], F32, kind="ExternalInput").ap()

    with ExitStack() as ctx:
        sb = lambda n_, s, d: ctx.enter_context(nc.sbuf_tensor("sb_" + n_, s, d))
        pst = lambda n_, s, d: ctx.enter_context(nc.psum_tensor("ps_" + n_, s, d))
        semh = {name: ctx.enter_context(nc.semaphore(name))
                for name in ("dc", "dk", "dx0", "dx1", "dg0", "dg1", "dd0", "dd1",
                             "dp0", "dp1", "st0", "st1", "T", "V", "S", "G", "Gid")}

        wq = sb("wq", [128, NPAIR * 128], BF16)
        wv = sb("wv", [128, NPAIR * 128], BF16)
        wk = sb("wk", [128, NE * 64], BF16)
        wo = sb("wo", [128, 64], BF16)
        bq = sb("bq", [128, NPAIR], F32)
        bk = sb("bk", [128, NE], F32)
        bv = sb("bv", [128, NPAIR], F32)
        bo = sb("bo", [128, 1], F32)
        gss = sb("gss", [128, NJ], F32)
        uss = sb("uss", [128, NJ], F32)
        dss = sb("dss", [128, NHH], F32)
        dsh = sb("dsh", [128, NHH], F32)
        gsh = sb("gsh", [128, NJ], F32)
        ident = sb("ident", [128, 128], BF16)
        xld = [sb(f"xld{i}", [128, D], BF16) for i in range(2)]
        xnT = sb("xnT", [128, NHH * HT], BF16)
        kdup = sb("kdup", [128, NE * CK], BF16)
        qp = [sb(f"qp{i}", [128, CK], BF16) for i in range(2)]
        vpb = [sb(f"vp{i}", [128, CK], BF16) for i in range(2)]
        sping = sb("sping", [128, 4096], BF16)
        Et = [sb(f"Et{i}", [128, NE * CK], BF16) for i in range(2)]
        zscr = sb("zscr", [128, 2048], BF16)
        Zf = sb("Zf", [128, CK], F32)
        Zr = sb("Zr", [128, CK], F32)
        wP = [sb(f"wP{i}", [128, CK], BF16) for i in range(2)]
        utmp = sb("utmp", [128, 2048], BF16)
        ctxa = sb("ctxa", [128, NE * CK], BF16)
        s_sb = sb("s_sb", [128, NPAIR * HT], BF16)
        wbuf = sb("wbuf", [128, 11264], BF16)
        sgt = [sb(f"sgt{i}", [128, HT], BF16) for i in range(2)]
        zgt = [sb(f"zgt{i}", [128, HT], BF16) for i in range(2)]
        sut = [sb(f"sut{i}", [128, HT], BF16) for i in range(2)]
        hdnT = sb("hdnT", [128, NJ * HT], BF16)
        tht = [sb(f"tht{i}", [128, HT], BF16) for i in range(2)]
        zt = [sb(f"zt{i}", [128, HT], BF16) for i in range(2)]
        xpc = [sb(f"xpc{i}", [128, HT], F32) for i in range(2)]
        opc = [sb(f"opc{i}", [128, 128], F32) for i in range(8)]
        lntr = sb("lntr", [128, D], BF16)
        stt1 = [sb(f"stt1_{i}", [128, 1], F32) for i in range(2)]
        stt2 = [sb(f"stt2_{i}", [128, 1], F32) for i in range(2)]
        ln_t = [sb(f"lnt{i}", [128, 1], F32) for i in range(6)]

        gw = [bass.AP(wbuf, 0, [[11264, 128], [1, 2048]]),
              bass.AP(wbuf, 2048, [[11264, 128], [1, 2048]])]
        uw = [bass.AP(wbuf, 4096, [[11264, 128], [1, 2048]]),
              bass.AP(wbuf, 6144, [[11264, 128], [1, 2048]])]
        dwb = [bass.AP(wbuf, 0, [[11264, 128], [1, 5632]]),
               bass.AP(wbuf, 5632, [[11264, 128], [1, 5632]])]

        pf = [pst(f"pf{i}", [128, 512], F32) for i in range(6)]
        pb = [pst(f"pb{i}", [128, 1024], BF16) for i in range(2)]

        pl = Plan()
        Q = pl.op
        M = pl.marks

        def dma_group(eng, stream, pairs, waits=()):
            allw = [(stream, pl.n(stream))] + list(waits)
            first = True
            for dst, src in pairs:
                Q(eng, (lambda e, d=dst, s=src: e.dma_start(out=d, in_=s)),
                  waits=tuple(allw) if first else (), inc=stream, amt=16)
                first = False
            return pl.n(stream)

        def blk64(t_, rows, colhalf):
            r = t_[rows * 64:(rows + 1) * 64, :].rearrange("p (b c) -> p b c", b=16)
            return r[:, :, colhalf * 64:(colhalf + 1) * 64]
        def src3(d_):
            return d_.rearrange("p (b c) -> p b c", b=16)
        M[("lnld", 0)] = dma_group("gpsimd", "dx0",
                                   [(xld[0][:], x_d[0:128, :])])
        M[("lnld", 1)] = dma_group("gpsimd", "dx1",
                                   [(xld[1][:], x_d[128:256, :])])
        Q("vector", lambda v: (v.memset(wq[:], 0.0), v.memset(wv[:], 0.0),
                               v.memset(wk[:], 0.0))[0], inc="V")
        VMSW = pl.n("V")
        CW = dma_group("gpsimd", "dk", [
            (blk64(wq, 0, 0), src3(wq_d[0])), (blk64(wq, 1, 1), src3(wq_d[1])),
            (blk64(wv, 0, 0), src3(wv_d[0])), (blk64(wv, 1, 1), src3(wv_d[1])),
            (blk64(wk, 0, 0), src3(wk_d[0])), (blk64(wk, 1, 1), src3(wk_d[1])),
            (wo[0:64, :], wo_d), (wo[64:128, :], wo_d)],
            waits=(("V", VMSW),))
        CX = dma_group("sync", "dc", [
            (bq[:], bq_d), (bk[:], bk_d), (bv[:], bv_d), (bo[:], bo_d),
            (gss[:], gs_d), (uss[:], us_d), (dss[:], ds_d)])
        Q("vector", lambda v: v.memset(ident[:], 0.0), inc="V")
        VMS = pl.n("V")
        Q("gpsimd", lambda g: g.affine_select(
            out=ident[:], in_=ident[:], compare_op=OP.not_equal, fill=1.0,
            base=0, pattern=[[-1, 128]], channel_multiplier=1),
          waits=(("V", VMS),), inc="Gid")
        GID = pl.n("Gid")
        Q("vector", lambda v: v.tensor_scalar(dsh[:], dss[:], 0.5, None, OP.mult),
          waits=(("dc", CX),), inc="V")
        Q("vector", lambda v: v.tensor_scalar(gsh[:], gss[:], 0.5, None, OP.mult),
          inc="V")
        V_DSH = pl.n("V")

        pl.marks[("stage", "consts")] = (len(pl.ops), dict(pl.cnt))
        xn_v = {}
        gu_mm = {}
        gu_sil = {}
        gu_ev = {}
        dn_mm = {}
        dn_th = {}
        fin_v = {}
        st_after = {}
        pb_guard = [0, 0]
        pb_sguard = [0, 0]
        akb = [0, 0]
        akbS = [0, 0]
        out_dmas = 0
        dn_ev = {}
        dn_sread = {}
        dn_txp = {}
        sev_pair = {}
        emit_down_parts = {}
        s_ready_map = {0: 0, 1: 0}
        emit_down_map = {}
        v_attn_end = 0
        t_xnT_read = 0
        t_hdn_read = 0

        def emit_ln_stats(h, i):
            gi = h * NLN + i
            sl = i % 2
            w = []
            if gi >= 2:
                w.append(("V", xn_v.get(gi - 2, 0)))
                w.append(("T", M.get(("lntx", gi - 2), 0)))
            xs = f"dx{sl}"
            if ("lnld", gi) in M:
                ldx = M.pop(("lnld", gi))
            else:
                ldx = dma_group("gpsimd", xs,
                                [(xld[sl][:], x_d[gi * 128:(gi + 1) * 128, :])],
                                waits=tuple(w))
            w = [(xs, ldx)]
            if gi >= 2:
                w.append(("V", xn_v.get(gi - 2, 0)))
            Q("scalar", (lambda s, s_=sl: s.activation(
                lntr[:], xld[s_][:], ACT.Square, accum_out=stt2[s_][:])),
                waits=tuple(w), inc="S")
            Q("scalar", (lambda s, s_=sl: s.activation(
                lntr[:], xld[s_][:], ACT.Identity,
                accum_out=stt1[s_][:])), inc="S")
            sacc = pl.n("S")

            m_, q2_, ve_, rc_, rs_, tp_ = ln_t
            chain = [
                (lambda v, s_=sl: v.tensor_scalar(m_[:], stt1[s_][:], 1.0 / D,
                                                  None, OP.mult)),
                (lambda v, s_=sl: v.tensor_scalar(q2_[:], stt2[s_][:], 1.0 / D,
                                                  None, OP.mult)),
                (lambda v: v.tensor_tensor(tp_[:], m_[:], m_[:], OP.mult)),
                (lambda v: v.tensor_tensor(ve_[:], q2_[:], tp_[:], OP.subtract)),
                (lambda v: v.tensor_scalar(ve_[:], ve_[:], 1.0, EPS,
                                           OP.mult, OP.add)),
                (lambda v: v.tensor_scalar(rs_[:], ve_[:], -0.5, 1.5,
                                           OP.mult, OP.add)),
            ]
            for it in range(3):
                chain += [
                    (lambda v: v.tensor_tensor(tp_[:], rs_[:], rs_[:], OP.mult)),
                    (lambda v: v.tensor_tensor(tp_[:], tp_[:], ve_[:], OP.mult)),
                    (lambda v: v.tensor_scalar(tp_[:], tp_[:], -0.5, 1.5,
                                               OP.mult, OP.add)),
                    (lambda v: v.tensor_tensor(rs_[:], rs_[:], tp_[:], OP.mult)),
                ]
            chain += [
                (lambda v: v.tensor_tensor(tp_[:], m_[:], rs_[:], OP.mult)),
                (lambda v: v.tensor_scalar(tp_[:], tp_[:], -1.0, None, OP.mult)),
                (lambda v, s_=sl: v.tensor_scalar(xld[s_][:], xld[s_][:],
                                                  rs_[:], tp_[:],
                                                  OP.mult, OP.add)),
            ]
            for cf in chain:
                Q("vector", cf,
                  waits=(("S", sacc), ("V", pl.n("V"))), inc="V",
                  keep_self=True)
            xn_v[gi] = pl.n("V")

        def emit_ln_evac(h, i):
            gi = h * NLN + i
            sl = i % 2
            for bat in range(2):
                pbk = pb[bat]
                wt = [("V", xn_v[gi]), ("Gid", GID), ("V", pb_guard[bat]),
                      ("S", pb_sguard[bat])]

                def xput(t, s_=sl, b=bat, pbk_=pbk):
                    last = None
                    for bb in range(8):
                        blk = b * 8 + bb
                        last = t.transpose(pbk_[:, bb * 128:(bb + 1) * 128],
                                           xld[s_][:, blk * 128:(blk + 1) * 128],
                                           ident[:])
                    return last
                Q("tensor", xput, waits=tuple(wt), inc="T")
                tx = pl.n("T")
                M[("lntx", gi)] = tx

                def xev(v, ii=i, b=bat, pbk_=pbk):
                    last = None
                    for bb in range(8):
                        blk = b * 8 + bb
                        last = v.tensor_copy(
                            xnT[:, blk * HT + ii * 128: blk * HT + (ii + 1) * 128],
                            pbk_[:, bb * 128:(bb + 1) * 128])
                    return last
                wv_ = [("T", tx)]
                if h == 1 and i < NLN - 1:
                    wv_.append(("T", gu_mm[(0, NJ - 1)]))
                else:
                    wv_.append(("T", t_xnT_read))
                Q("vector", xev, waits=tuple(wv_), inc="V")
                pb_guard[bat] = pl.n("V")
            M[("xnTt", h, i)] = pl.n("V")
            pl.marks[("stage", f"lnt{gi}")] = (len(pl.ops), dict(pl.cnt))

        for h in range(2):
            if h == 0:
                emit_ln_stats(0, 0)
                emit_ln_evac(0, 0)
            pl.marks[("stage", f"ln{h}")] = (len(pl.ops), dict(pl.cnt))

            if h == 0:
                Q("scalar", lambda s: s.preload_act_table(ACT.Exp),
                  waits=(("S", pl.n("S")),))

            def emit_gu(j):
                sl = j % 2
                w = []
                if j >= 2:
                    w.append(("T", gu_mm[(h, j - 2)]))
                elif h == 1:
                    w.append(("T", dn_mm[(0, NHH - 1)]))
                gs_ = f"dg{sl}"
                wld = dma_group("gpsimd", gs_,
                                [(gw[sl], gwb_d[j]), (uw[sl], uwb_d[j])],
                                waits=tuple(w))
                pg, pu = pf[0 + sl * 2], pf[1 + sl * 2]
                w = [(gs_, wld), ("V", M[("xnTt", h, NLN - 1)])]
                if j >= 2:
                    w.append(("V", gu_ev[(h, j - 2)]))
                    w.append(("S", gu_sil[(h, j - 2)]))
                elif h == 1:
                    w.append(("V", dn_ev[(0, NHH - 1)]))
                    w.append(("S", dn_th[(0, NHH - 1)]))

                def gumm(t, s_=sl, pg_=pg, pu_=pu):
                    for k in range(NHH):
                        t.matmul(pg_[:], gw[s_][:, k * 128:(k + 1) * 128],
                                 xnT[:, k * HT:(k + 1) * HT],
                                 start=(k == 0), stop=(k == NHH - 1))
                    last = None
                    for k in range(NHH):
                        last = t.matmul(pu_[:], uw[s_][:, k * 128:(k + 1) * 128],
                                        xnT[:, k * HT:(k + 1) * HT],
                                        start=(k == 0), stop=(k == NHH - 1))
                    return last
                Q("tensor", gumm, waits=tuple(w), inc="T")
                gu_mm[(h, j)] = pl.n("T")
                Q("scalar", (lambda s, jj=j, s_=sl, pg_=pg: s.activation(
                    sgt[s_][:], pg_[:], ACT.Tanh, scale=gsh[:, jj:jj + 1])),
                    waits=(("T", gu_mm[(h, j)]), ("V", V_DSH)), inc="S")
                Q("scalar", (lambda s, jj=j, s_=sl, pg_=pg: s.activation(
                    zgt[s_][:], pg_[:], ACT.Identity, scale=gsh[:, jj:jj + 1])),
                    inc="S")
                Q("scalar", (lambda s, jj=j, s_=sl, pu_=pu: s.activation(
                    sut[s_][:], pu_[:], ACT.Identity, scale=uss[:, jj:jj + 1])),
                    inc="S")
                gu_sil[(h, j)] = pl.n("S")
                whd = [("S", gu_sil[(h, j)])]
                if j == 0 and h == 1:
                    whd.append(("T", t_hdn_read))
                Q("vector", (lambda v, s_=sl: v.scalar_tensor_tensor(
                    zgt[s_][:], sgt[s_][:], 1.0, zgt[s_][:], OP.add, OP.mult)),
                    waits=tuple(whd), inc="V")
                Q("vector", (lambda v, jj=j, s_=sl: v.tensor_tensor(
                    hdnT[:, jj * HT:(jj + 1) * HT], zgt[s_][:], sut[s_][:], OP.mult)),
                    inc="V")
                gu_ev[(h, j)] = pl.n("V")

            def emit_attn(c, filler=None, per=8, per_z=0, skip_k=False,
                          next_k=None):
                it = attn_stepper(c, skip_k=skip_k, next_k=next_k)
                emitted = 0
                for i, tag in enumerate(it):
                    if filler is None:
                        continue
                    if tag[0] == "P":
                        want = ((tag[1] + 1) * per) // NPAIR
                        filler(want - emitted)
                        emitted = want
                    elif tag[0] == "z":
                        filler(per_z)

            def emit_kphase(hh, c):
                tc = c * CK
                for bat in range(8):
                    b = bat % 2
                    ps_ = pf[4 + b]
                    w = [("V", akb[b]), ("S", akbS[b]), ("V", M[("xnTt", hh, c)]),
                         ("dk", CW)]

                    def kmm(t, bt=bat, ps=ps_, tc_=tc):
                        last = None
                        for q4 in range(4):
                            e = bt * 4 + q4
                            jj = e // 2
                            lhs = wk[:, e * 64:(e + 1) * 64]
                            rhs = xnT[:, jj * HT + tc_: jj * HT + tc_ + CK]
                            t.matmul(ps[0:64, q4 * 128:(q4 + 1) * 128], lhs, rhs,
                                     start=True, stop=True)
                            last = t.matmul(ps[64:128, q4 * 128:(q4 + 1) * 128],
                                            lhs, rhs, start=True, stop=True)
                        return last
                    Q("tensor", kmm, waits=tuple(w), inc="T")
                    tk = pl.n("T")

                    def kev(s, bt=bat, ps=ps_):
                        last = None
                        for q4 in range(4):
                            e = bt * 4 + q4
                            last = s.activation(
                                kdup[:, e * 128:(e + 1) * 128],
                                ps[:, q4 * 128:(q4 + 1) * 128],
                                ACT.Identity, bias=bk[:, e:e + 1])
                        return last
                    Q("scalar", kev,
                      waits=(("T", tk), ("V", M.get(("vs_last",), 0))), inc="S")
                    akbS[b] = pl.n("S")
                M[("kready", hh, c)] = pl.n("S")

            def attn_stepper(c, womm_cb=None, skip_k=False, next_k=None):
                tc = c * CK
                if not skip_k:
                    emit_kphase(h, c)
                kdup_ready_s = M[("kready", h, c)]
                yield ("k", c)

                def do_qse(P):
                    sl = P % 2
                    w = [("V", akb[0]), ("V", akb[1]), ("S", akbS[0]), ("S", akbS[1])]

                    def qvmm(t, Pp=P, tc_=tc):
                        rhs = xnT[:, Pp * HT + tc_: Pp * HT + tc_ + CK]
                        t.matmul(pf[4][0:64, 0:128],
                                 wq[:, Pp * 128: Pp * 128 + 64], rhs,
                                 start=True, stop=True)
                        t.matmul(pf[4][64:128, 0:128],
                                 wq[:, Pp * 128 + 64:(Pp + 1) * 128], rhs,
                                 start=True, stop=True)
                        t.matmul(pf[5][0:64, 0:128],
                                 wv[:, Pp * 128: Pp * 128 + 64], rhs,
                                 start=True, stop=True)
                        return t.matmul(pf[5][64:128, 0:128],
                                        wv[:, Pp * 128 + 64:(Pp + 1) * 128], rhs,
                                        start=True, stop=True)
                    Q("tensor", qvmm, waits=tuple(w), inc="T")
                    tqv = pl.n("T")

                    def qvev(s, Pp=P, s_=sl):
                        s.activation(qp[s_][:], pf[4][:, 0:128], ACT.Identity,
                                     bias=bq[:, Pp:Pp + 1])
                        return s.activation(vpb[s_][:], pf[5][:, 0:128],
                                            ACT.Identity, bias=bv[:, Pp:Pp + 1])
                    Q("scalar", qvev, waits=(("T", tqv),), inc="S")
                    akbS[0] = akbS[1] = pl.n("S")
                    qvdone_s = pl.n("S")

                    w = [("S", qvdone_s), ("S", kdup_ready_s),
                         ("S", M.get(("exps", 0), 0))]

                    def smul(v, s_=sl):
                        qrep = bass.AP(qp[s_], 0, [[CK, 128], [0, 32], [1, CK]])
                        return v.tensor_tensor(
                            sping[:].rearrange("p (r n) -> p r n", r=32),
                            qrep,
                            kdup[:].rearrange("p (r n) -> p r n", r=32),
                            OP.mult)
                    Q("vector", smul, waits=tuple(w), inc="V")
                    vs = pl.n("V")
                    M[("vs_last",)] = vs
                    Q("scalar", (lambda s, s_=sl: s.activation(
                        Et[s_][:], sping[:], ACT.Exp, scale=0.125)),
                        waits=(("V", vs), ("V", M.get(("uctxp", sl), 0))),
                        inc="S")
                    M[("exps", 0)] = pl.n("S")
                    M[("expdone", P % 2)] = pl.n("S")

                def do_zu(P):
                    sl = P % 2

                    def ztree(v, s_=sl):
                        ev = bass.AP(Et[s_], 0, [[NE * CK, 128], [256, 16], [1, 128]])
                        od = bass.AP(Et[s_], 128, [[NE * CK, 128], [256, 16], [1, 128]])
                        v.tensor_tensor(
                            zscr[:, 0:2048].rearrange("p (r n) -> p r n", r=16),
                            ev, od, OP.add)
                        e2 = bass.AP(zscr, 0, [[2048, 128], [256, 8], [1, 128]])
                        o2 = bass.AP(zscr, 128, [[2048, 128], [256, 8], [1, 128]])
                        v.tensor_tensor(
                            zscr[:, 0:1024].rearrange("p (r n) -> p r n", r=8),
                            e2, o2, OP.add)
                        e3 = bass.AP(zscr, 0, [[2048, 128], [256, 4], [1, 128]])
                        o3 = bass.AP(zscr, 128, [[2048, 128], [256, 4], [1, 128]])
                        v.tensor_tensor(
                            zscr[:, 0:512].rearrange("p (r n) -> p r n", r=4),
                            e3, o3, OP.add)
                        e4 = bass.AP(zscr, 0, [[2048, 128], [256, 2], [1, 128]])
                        o4 = bass.AP(zscr, 128, [[2048, 128], [256, 2], [1, 128]])
                        v.tensor_tensor(
                            zscr[:, 0:256].rearrange("p (r n) -> p r n", r=2),
                            e4, o4, OP.add)
                        v.tensor_tensor(Zf[:], zscr[:, 0:128], zscr[:, 128:256],
                                        OP.add)
                        v.reciprocal(Zr[:], Zf[:])
                        return v.tensor_tensor(wP[s_][:], vpb[s_][:], Zr[:], OP.mult)
                    Q("vector", ztree, waits=(("S", M[("expdone", sl)]),), inc="V")
                    wready = pl.n("V")

                    def uctx(v, Pp=P, s_=sl):
                        last = None
                        for mg in range(2):
                            wrep = bass.AP(wP[s_], 0, [[CK, 128], [0, 16], [1, CK]])
                            esl = Et[s_][:, mg * 2048:(mg + 1) * 2048].rearrange(
                                "p (r n) -> p r n", r=16)
                            csl = ctxa[:, mg * 2048:(mg + 1) * 2048].rearrange(
                                "p (r n) -> p r n", r=16)
                            if Pp == 0:
                                last = v.tensor_tensor(csl, wrep, esl, OP.mult)
                            else:
                                ut = utmp[:].rearrange("p (r n) -> p r n", r=16)
                                v.tensor_tensor(ut, wrep, esl, OP.mult)
                                last = v.tensor_tensor(csl, csl, ut, OP.add)
                        return last
                    Q("vector", uctx, waits=(("V", wready),), inc="V")
                    M[("uctxp", sl)] = pl.n("V")

                do_qse(0)
                yield ("P", 0)
                for P in range(1, NPAIR):
                    do_qse(P)
                    if P == NPAIR - 1 and next_k is not None:
                        emit_kphase(*next_k)
                    do_zu(P - 1)
                    yield ("P", P)
                do_zu(NPAIR - 1)
                pctx = pl.n("V")
                yield ("z", c)

                for grp in range(4):
                    b = grp % 2
                    ps_ = pf[4 + b]
                    jps = list(range(grp * 4, grp * 4 + 4))
                    w = [("V", pctx), ("S", akbS[b]), ("V", akb[b])]

                    def womm(t, jj=tuple(jps), ps=ps_):
                        last = None
                        for q, j_ in enumerate(jj):
                            me, mo = 2 * j_, 2 * j_ + 1
                            t.matmul(ps[0:64, q * 128:(q + 1) * 128], wo[:, 0:64],
                                     ctxa[:, me * 128:(me + 1) * 128],
                                     start=True, stop=True)
                            last = t.matmul(ps[64:128, q * 128:(q + 1) * 128],
                                            wo[:, 0:64],
                                            ctxa[:, mo * 128:(mo + 1) * 128],
                                            start=True, stop=True)
                        return last
                    Q("tensor", womm, waits=tuple(w), inc="T")
                    tw = pl.n("T")

                    wsev = [("T", tw)] + [("V", dn_sread.get((h - 1, j_), 0))
                                          for j_ in jps]

                    def sev(s, jj=tuple(jps), tc_=tc, ps=ps_):
                        last = None
                        for q, j_ in enumerate(jj):
                            last = s.activation(
                                s_sb[:, j_ * HT + tc_: j_ * HT + tc_ + CK],
                                ps[:, q * 128:(q + 1) * 128], ACT.Identity,
                                bias=bo[:])
                        return last
                    Q("scalar", sev, waits=tuple(wsev), inc="S")
                    akbS[b] = pl.n("S")
                    akb[b] = max(akb[b], pctx)
                    for j_ in jps:
                        sev_pair[(h, j_)] = pl.n("S")
                    if womm_cb is not None:
                        for j_ in jps:
                            womm_cb(j_)
                s_ready_map[h] = pl.n("S")
                yield ("end", c)

            def emit_xpc_load(hd, hh, eng="sync", pfx="dp"):
                ghh = hd * NHH + hh
                sl = hh % 2
                wx = []
                if ghh >= 2:
                    wx.append(("V", fin_v.get(ghh - 2, 0)))
                ps_str = f"{pfx}{sl}"
                xl = dma_group(eng, ps_str, [(
                    xpc[sl][:].rearrange("p (a b) -> p a b", a=4),
                    bass.AP(x_h, (hd * HT) * D + hh * 128,
                            [[D, 128], [128 * D, 4], [1, 128]]))],
                    waits=tuple(wx))
                M[("dnxl", ghh)] = (ps_str, xl)

            def emit_down_mm(hd, hh, xpc_at_mm=True):
                ghh = hd * NHH + hh
                sl = hh % 2
                w = [("T", gu_mm[(hd, NJ - 1)])]
                if hh >= 2:
                    w.append(("T", dn_mm[(hd, hh - 2)]))
                ds_ = f"dd{sl}"
                dld = dma_group("gpsimd", ds_, [(dwb[sl], dwb_d[hh])],
                                waits=tuple(w))
                if xpc_at_mm:
                    emit_xpc_load(hd, hh)
                pd = pf[hh % 4]
                w = [(ds_, dld), ("V", gu_ev[(hd, NJ - 1)])]
                if hh >= 4:
                    w.append(("S", dn_th[(hd, hh - 4)]))
                    w.append(("V", dn_ev[(hd, hh - 4)]))
                else:
                    w.append(("S", gu_sil[(hd, NJ - 1)]))
                    w.append(("V", gu_ev[(hd, NJ - 1)]))

                def dnmm(t, s_=sl, pd_=pd):
                    last = None
                    for j in range(NJ):
                        last = t.matmul(pd_[:], dwb[s_][:, j * 128:(j + 1) * 128],
                                        hdnT[:, j * HT:(j + 1) * HT],
                                        start=(j == 0), stop=(j == NJ - 1))
                    return last
                Q("tensor", dnmm, waits=tuple(w), inc="T")
                dn_mm[(hd, hh)] = pl.n("T")

            def emit_down_evac(hd, hh, zslot=None, fin=True):
                sl = hh % 2
                pd = pf[hh % 4]
                zs = zslot if zslot is not None else zt[sl][:]
                Q("scalar", (lambda s, k=hh, s_=sl, pd_=pd: s.activation(
                    tht[s_][:], pd_[:], ACT.Tanh, scale=dsh[:, k:k + 1])),
                    waits=(("T", dn_mm[(hd, hh)]), ("V", V_DSH)), inc="S")
                dn_th[(hd, hh)] = pl.n("S")
                wv_ = [("S", dn_th[(hd, hh)])]
                if hd == 1 and zslot is not None:
                    wv_.append(("T", M.get(("lntx", 2 * NLN - 1), 0)))

                def dnev(v, k=hh, s_=sl, pd_=pd, zs_=zs):
                    v.tensor_scalar(zs_, pd_[:], dsh[:, k:k + 1], None, OP.mult)
                    return v.scalar_tensor_tensor(zs_, tht[s_][:], 1.0, zs_,
                                                  OP.add, OP.mult)
                Q("vector", dnev, waits=tuple(wv_), inc="V")
                dn_ev[(hd, hh)] = pl.n("V")
                if fin:
                    emit_down_fin(hd, hh, zslot=zs)

            def emit_down_fin(hd, hh, zslot=None):
                nonlocal out_dmas
                ghh = hd * NHH + hh
                sl = hh % 2
                zs = zslot if zslot is not None else zt[sl][:]
                if ("dnxl", ghh) not in M:
                    emit_xpc_load(hd, hh, eng="gpsimd", pfx="dx")
                ps_str, xl = M[("dnxl", ghh)]

                def dfin(v, k=hh, s_=sl, zs_=zs):
                    return v.tensor_tensor(tht[s_][:], zs_,
                                           s_sb[:, k * HT:(k + 1) * HT], OP.mult)
                Q("vector", dfin, waits=(("V", dn_ev[(hd, hh)]),
                                         ("S", sev_pair.get((hd, hh), 0)),
                                         ("T", dn_txp.get((hd, hh - 2), 0))),
                  inc="V")
                vP_ = pl.n("V")
                dn_sread[(hd, hh)] = vP_
                pbk = pb[hh % 2]
                wt = [("V", vP_), ("Gid", GID), ("V", pb_guard[hh % 2]),
                      ("S", pb_sguard[hh % 2])]

                def pxp(t, s_=sl, pbk_=pbk):
                    last = None
                    for tb in range(4):
                        last = t.transpose(pbk_[:, tb * 128:(tb + 1) * 128],
                                           tht[s_][:, tb * 128:(tb + 1) * 128],
                                           ident[:])
                    return last
                Q("tensor", pxp, waits=tuple(wt), inc="T")
                txp = pl.n("T")
                dn_txp[(hd, hh)] = txp

                def finadd(v, s_=sl, pbk_=pbk, par=(hh % 2)):
                    last = None
                    for tb in range(4):
                        last = v.tensor_tensor(opc[par * 4 + tb][:],
                                               pbk_[:, tb * 128:(tb + 1) * 128],
                                               xpc[s_][:, tb * 128:(tb + 1) * 128],
                                               OP.add)
                    return last
                Q("vector", finadd,
                  waits=(("T", txp), (ps_str, xl),
                         (f"st{hh % 2}", st_after.get(ghh - 2, 0))), inc="V")
                fin_v[ghh] = pl.n("V")
                pb_guard[hh % 2] = fin_v[ghh]
                ss_ = f"st{hh % 2}"
                dma_group("sync", ss_, [(
                    bass.AP(out_h, (hd * HT + tb * 128) * D + hh * 128,
                            [[D, 128], [1, 128]]),
                    opc[(hh % 2) * 4 + tb][:]) for tb in range(4)],
                    waits=(("V", fin_v[ghh]),))
                out_dmas += 4
                st_after[ghh] = pl.n(ss_)

            def emit_down_hh(hd, hh):
                emit_down_mm(hd, hh)
                emit_down_evac(hd, hh)

            emit_down_map[h] = emit_down_hh
            emit_down_parts[h] = (emit_down_mm, emit_down_evac)

            units = []
            if h == 0:
                units += [(lambda ii=i: (emit_ln_stats(0, ii),
                                         emit_ln_evac(0, ii)))
                          for i in (1, 2, 3)]
            units += [(lambda jj=j: emit_gu(jj)) for j in range(NJ)]
            if h == 0:
                for i in range(NLN):
                    units.append(lambda ii=i: emit_ln_stats(1, ii))
                    if i < NLN - 1:
                        units.append(lambda ii=i: emit_ln_evac(1, ii))
            uq = iter(units)
            nu = len(units)

            def drain(n):
                for _ in range(n):
                    u = next(uq, None)
                    if u is not None:
                        u()
            if h == 0:
                for c, per in zip(range(NCH), (11, 13, 13, 17)):
                    emit_attn(c, filler=drain, per=per)
                drain(nu)
            else:
                g = attn_stepper(0)
                edh = emit_down_map[0]
                dn0 = iter(range(NHH))
                for tag in g:
                    if tag[0] == "k":
                        emit_ln_evac(1, NLN - 1)
                    elif tag[0] == "P":
                        hh = next(dn0, None)
                        if hh is not None:
                            edh(0, hh)
                t_hdn_read = pl.n("T")
                emit_attn(1, filler=drain, per=22)
                emit_attn(2, filler=drain, per=22)
                drain(nu)
                dn1_zsl = ([xld[0][:, q * HT:(q + 1) * HT] for q in range(4)]
                           + [xld[1][:, q * HT:(q + 1) * HT] for q in range(4)]
                           + [sgt[0][:], sgt[1][:], zgt[0][:], zgt[1][:],
                              sut[0][:], sut[1][:], zt[0][:], zt[1][:]])
                dn1 = iter(range(NHH))
                dn1_pend = []

                def dn1_step():
                    if dn1_pend:
                        hhp = dn1_pend.pop(0)
                        emit_down_evac(1, hhp, zslot=dn1_zsl[hhp], fin=False)
                    hh2 = next(dn1, None)
                    if hh2 is not None:
                        emit_down_mm(1, hh2, xpc_at_mm=False)
                        dn1_pend.append(hh2)
                for tag in attn_stepper(3):
                    if tag[0] == "P":
                        dn1_step()
                while True:
                    if dn1_pend:
                        hh2 = dn1_pend.pop(0)
                        emit_down_evac(1, hh2, zslot=dn1_zsl[hh2], fin=False)
                        continue
                    hh2 = next(dn1, None)
                    if hh2 is None:
                        break
                    emit_down_mm(1, hh2, xpc_at_mm=False)
                    dn1_pend.append(hh2)
                for hh2 in range(NHH):
                    emit_down_fin(1, hh2, zslot=dn1_zsl[hh2])
            v_attn_end = pl.n("V")
            t_xnT_read = pl.n("T")

            pl.marks[("stage", f"attn{h}")] = (len(pl.ops), dict(pl.cnt))


            pl.marks[("stage", f"down{h}")] = (len(pl.ops), dict(pl.cnt))

        global LAST_MARKS
        LAST_MARKS = {k2[1]: v[0] for k2, v in pl.marks.items()
                      if isinstance(k2, tuple) and k2[0] == "stage"}
        if stage is not None:
            if isinstance(stage, str) and stage.startswith("op"):
                n_cut = int(stage[2:])
                cc = {}
                for (e_, w_, f_, i_, a_) in pl.ops[:n_cut]:
                    if i_ is not None:
                        cc[i_] = cc.get(i_, 0) + a_
                pl.marks[("stage", stage)] = (n_cut, cc)
            nops, cnts = pl.marks[("stage", stage)]
            pl.ops = pl.ops[:nops]
            dma_owner = {"dc": "sync", "dp0": "sync", "dp1": "sync",
                         "st0": "sync", "st1": "sync",
                         "dk": "gpsimd", "dx0": "gpsimd", "dx1": "gpsimd",
                         "dg0": "gpsimd", "dg1": "gpsimd",
                         "dd0": "gpsimd", "dd1": "gpsimd"}
            for sname, eng in dma_owner.items():
                v = cnts.get(sname, 0)
                if v > 0:
                    pl.ops.append((eng, [(sname, v)], (lambda e: None), None, 1))
            dumps = [("sb_xld0", xld[0]), ("sb_xld1", xld[1]),
                     ("sb_lnt0", ln_t[0]), ("sb_lnt1", ln_t[1]), ("sb_lnt2", ln_t[2]), ("sb_lnt3", ln_t[3]), ("sb_lnt4", ln_t[4]), ("sb_lnt5", ln_t[5]), ("sb_stt1", stt1[0]), ("sb_stt2", stt2[0]),
                     ("sb_wq", wq), ("sb_wk", wk), ("sb_wv", wv), ("sb_xnT", xnT)]
            if not (stage.startswith("ln") or stage.startswith("consts")):
                dumps += [
                     ("sb_kdup", kdup), ("sb_qp0", qp[0]), ("sb_vp0", vpb[0]),
                     ("sb_sping", sping), ("sb_Et", Et), ("sb_Zf", Zf),
                     ("sb_ctxa", ctxa), ("sb_s_sb", s_sb), ("sb_hdnT", hdnT)]
            cw = [("T", cnts.get("T", 0)), ("V", cnts.get("V", 0)),
                  ("S", cnts.get("S", 0))]
            col = 0
            pairs = []
            dump_cols = {}
            for nm, t_ in dumps:
                ncol = t_.shape[1] if not isinstance(t_, bass.AP) else t_.shape[1]
                ncol = min(ncol, 8192)
                dump_cols[nm] = (col, ncol)
                pairs.append((bass.AP(dbg_h, col, [[65536, 128], [1, ncol]]),
                              t_[:, 0:ncol]))
                col += ncol
            global DUMP_COLS
            DUMP_COLS = dump_cols
            for idx, (d_, s_) in enumerate(pairs):
                pl.ops.append(("gpsimd", cw if idx == 0 else [], (
                    lambda e, dd=d_, ss=s_: e.dma_start(out=dd, in_=ss, allow_slow_non_contiguous=True)), "st0", 16))
            pl.ops.append(("gpsimd", [("st0", cnts.get("st0", 0) + 16 * len(pairs))],
                           (lambda e: None), None, 1))
        else:
            Q("sync", lambda e: None,
              waits=(("st0", pl.n("st0")), ("st1", pl.n("st1"))))

        block = ctx.enter_context(nc.Block())

        def runner(engname):
            def body(eng):
                for (e, waits, emit, inc, amt) in pl.ops:
                    if e != engname:
                        continue
                    for (s, v) in waits:
                        eng.wait_ge(semh[s], v)
                    inst = emit(eng)
                    if inc is not None:
                        assert inst is not None, f"{engname} op with inc returned None"
                        inst.then_inc(semh[inc], amt)
            return body

        block.sync(runner("sync"))
        block.gpsimd(runner("gpsimd"))
        block.tensor(runner("tensor"))
        block.vector(runner("vector"))
        block.scalar(runner("scalar"))

    return nc


LAST_MARKS = {}
DUMP_COLS = {}
_NC_CACHE = None


def _get_nc():
    global _NC_CACHE
    if _NC_CACHE is None:
        _NC_CACHE = build_program()
    return _NC_CACHE


def kernel(L0_attn_x, L0_attn_Wqkv, L0_attn_bqkv, L0_attn_Wout, L0_attn_bout,
           L0_ffn_gate_w, L0_ffn_up_w, L0_ffn_down_w, L0_ffn_gate_s, L0_ffn_up_s,
           L0_ffn_down_s):
    asc = np.ascontiguousarray
    f32 = np.float32
    x = asc(np.asarray(L0_attn_x, f32).reshape(TOK * NCORES, D))
    Wqkv = np.asarray(L0_attn_Wqkv, f32)
    bqkv = np.asarray(L0_attn_bqkv, f32)
    Wout = asc(np.asarray(L0_attn_Wout, f32))
    bout = np.asarray(L0_attn_bout, f32)
    gate_w = np.asarray(L0_ffn_gate_w, f32)
    up_w = np.asarray(L0_ffn_up_w, f32)
    down_w = np.asarray(L0_ffn_down_w, f32)
    gate_s = np.asarray(L0_ffn_gate_s, f32)
    up_s = np.asarray(L0_ffn_up_s, f32)
    down_s = np.asarray(L0_ffn_down_s, f32)

    gate_blk = asc(gate_w.T.reshape(NHH, 128, NJ, 128).transpose(2, 1, 0, 3)
                   ).reshape(NJ, 128, NHH * 128)
    up_blk = asc(up_w.T.reshape(NHH, 128, NJ, 128).transpose(2, 1, 0, 3)
                 ).reshape(NJ, 128, NHH * 128)
    down_blk = asc(down_w.T.reshape(NJ, 128, NHH, 128).transpose(2, 1, 0, 3)
                   ).reshape(NHH, 128, NJ * 128)
    wq = asc(np.stack([Wqkv[0::2, :, 0:64].transpose(1, 0, 2).reshape(64, NPAIR * 64),
                       Wqkv[1::2, :, 0:64].transpose(1, 0, 2).reshape(64, NPAIR * 64)]))
    wv = asc(np.stack([Wqkv[0::2, :, 128:192].transpose(1, 0, 2).reshape(64, NPAIR * 64),
                       Wqkv[1::2, :, 128:192].transpose(1, 0, 2).reshape(64, NPAIR * 64)]))
    wk = asc(np.stack([Wqkv[0::2, :, 64:128].transpose(1, 0, 2).reshape(64, NPAIR * 64),
                       Wqkv[1::2, :, 64:128].transpose(1, 0, 2).reshape(64, NPAIR * 64)]))
    bq = asc(bqkv[:, 0:64].reshape(NPAIR, 2, 64).transpose(1, 2, 0).reshape(128, NPAIR))
    bkh = asc(np.concatenate([bqkv[:, 64:128].T, bqkv[:, 64:128].T], axis=0))
    bv = asc(bqkv[:, 128:192].reshape(NPAIR, 2, 64).transpose(1, 2, 0).reshape(128, NPAIR))
    bo = asc(np.concatenate([bout, bout])[:, None])
    gs = asc(gate_s.reshape(NJ, 128).T)
    us = asc(up_s.reshape(NJ, 128).T)
    ds = asc(down_s.reshape(NHH, 128).T)

    nc = _get_nc()
    shared = {
        "gate_blk": gate_blk, "up_blk": up_blk, "down_blk": down_blk,
        "wq": wq, "wk": wk, "wv": wv, "wo": Wout,
        "bq": bq, "bk": bkh, "bv": bv, "bo": bo,
        "gs": gs, "us": us, "ds": ds,
    }
    in_maps = []
    for c in range(NCORES):
        m = dict(shared)
        m["x"] = asc(x[c * TOK:(c + 1) * TOK])
        in_maps.append(m)
    res = run_bass_kernel_spmd(nc, in_maps, core_ids=list(range(NCORES)))
    out = np.concatenate([np.asarray(r["out"]) for r in res.results], axis=0)
    return out.reshape(4, 2048, 2048).astype(np.float32)


if __name__ == "__main__":
    print("building program...")
    nc = build_program()
    print("built ok")


# version 24
# speedup vs baseline: 1.0025x; 1.0025x over previous
import numpy as np
import concourse.bass as bass
import concourse.mybir as mybir
from concourse.bass_utils import run_bass_kernel_spmd
from contextlib import ExitStack

F32 = mybir.dt.float32
BF16 = mybir.dt.bfloat16
OP = mybir.AluOpType
ACT = mybir.ActivationFunctionType

NCORES = 8
TOK = 1024
D = 2048
HT = 512
NLN = 4
NJ = 44
NHH = 16
NPAIR = 16
NE = 32
CK = 128
NCH = HT // CK
EPS = 1e-5


class Plan:

    def __init__(self):
        self.ops = []
        self.cnt = {}
        self.waited = {}
        self.marks = {}

    def n(self, sem):
        return self.cnt.get(sem, 0)

    def op(self, eng, emit, waits=(), inc=None, amt=1, keep_self=False):
        pruned = []
        for (s, v) in waits:
            if v <= 0:
                continue
            assert v <= self.cnt.get(s, 0), f"forward wait {s}={v}>{self.cnt.get(s, 0)}"
            if _OWNER.get(s) == eng and not keep_self:
                continue
            k = (eng, s)
            if self.waited.get(k, 0) >= v:
                continue
            self.waited[k] = v
            pruned.append((s, v))
        if inc is not None:
            self.cnt[inc] = self.cnt.get(inc, 0) + amt
        self.ops.append((eng, pruned, emit, inc, amt))
        return self.cnt.get(inc, 0) if inc else None


_OWNER = {"T": "tensor", "V": "vector", "S": "scalar", "G": "gpsimd",
          "Gid": "gpsimd"}


def build_program(stage=None):
    nc = bass.Bass("TRN2", target_bir_lowering=False, detect_race_conditions=False)

    x_h = nc.dram_tensor("x", [TOK, D], F32, kind="ExternalInput")
    out_h = nc.dram_tensor("out", [TOK, D], F32, kind="ExternalOutput")
    dbg_h = (nc.dram_tensor("dbg", [128, 65536], F32, kind="ExternalOutput")
             if stage is not None else None)
    x_d = x_h.ap()
    gwb_d = nc.dram_tensor("gate_blk", [NJ, 128, NHH * 128], F32, kind="ExternalInput").ap()
    uwb_d = nc.dram_tensor("up_blk", [NJ, 128, NHH * 128], F32, kind="ExternalInput").ap()
    dwb_d = nc.dram_tensor("down_blk", [NHH, 128, NJ * 128], F32, kind="ExternalInput").ap()
    wq_d = nc.dram_tensor("wq", [2, 64, NPAIR * 64], F32, kind="ExternalInput").ap()
    wk_d = nc.dram_tensor("wk", [2, 64, NPAIR * 64], F32, kind="ExternalInput").ap()
    wv_d = nc.dram_tensor("wv", [2, 64, NPAIR * 64], F32, kind="ExternalInput").ap()
    wo_d = nc.dram_tensor("wo", [64, 64], F32, kind="ExternalInput").ap()
    bq_d = nc.dram_tensor("bq", [128, NPAIR], F32, kind="ExternalInput").ap()
    bk_d = nc.dram_tensor("bk", [128, NE], F32, kind="ExternalInput").ap()
    bv_d = nc.dram_tensor("bv", [128, NPAIR], F32, kind="ExternalInput").ap()
    bo_d = nc.dram_tensor("bo", [128, 1], F32, kind="ExternalInput").ap()
    gs_d = nc.dram_tensor("gs", [128, NJ], F32, kind="ExternalInput").ap()
    us_d = nc.dram_tensor("us", [128, NJ], F32, kind="ExternalInput").ap()
    ds_d = nc.dram_tensor("ds", [128, NHH], F32, kind="ExternalInput").ap()

    with ExitStack() as ctx:
        sb = lambda n_, s, d: ctx.enter_context(nc.sbuf_tensor("sb_" + n_, s, d))
        pst = lambda n_, s, d: ctx.enter_context(nc.psum_tensor("ps_" + n_, s, d))
        semh = {name: ctx.enter_context(nc.semaphore(name))
                for name in ("dc", "dk", "dx0", "dx1", "dg0", "dg1", "dd0", "dd1",
                             "dp0", "dp1", "st0", "st1", "T", "V", "S", "G", "Gid")}

        wq = sb("wq", [128, NPAIR * 128], BF16)
        wv = sb("wv", [128, NPAIR * 128], BF16)
        wk = sb("wk", [128, NE * 64], BF16)
        wo = sb("wo", [128, 64], BF16)
        bq = sb("bq", [128, NPAIR], F32)
        bk = sb("bk", [128, NE], F32)
        bv = sb("bv", [128, NPAIR], F32)
        bo = sb("bo", [128, 1], F32)
        gss = sb("gss", [128, NJ], F32)
        uss = sb("uss", [128, NJ], F32)
        dss = sb("dss", [128, NHH], F32)
        dsh = sb("dsh", [128, NHH], F32)
        gsh = sb("gsh", [128, NJ], F32)
        ident = sb("ident", [128, 128], BF16)
        xld = [sb(f"xld{i}", [128, D], BF16) for i in range(2)]
        xnT = sb("xnT", [128, NHH * HT], BF16)
        kdup = sb("kdup", [128, NE * CK], BF16)
        qp = [sb(f"qp{i}", [128, CK], BF16) for i in range(2)]
        vpb = [sb(f"vp{i}", [128, CK], BF16) for i in range(2)]
        sping = sb("sping", [128, 4096], BF16)
        Et = [sb(f"Et{i}", [128, NE * CK], BF16) for i in range(2)]
        zscr = sb("zscr", [128, 2048], BF16)
        Zf = sb("Zf", [128, CK], F32)
        Zr = sb("Zr", [128, CK], F32)
        wP = [sb(f"wP{i}", [128, CK], BF16) for i in range(2)]
        utmp = sb("utmp", [128, 2048], BF16)
        ctxa = sb("ctxa", [128, NE * CK], BF16)
        s_sb = sb("s_sb", [128, NPAIR * HT], BF16)
        wbuf = sb("wbuf", [128, 11264], BF16)
        sgt = [sb(f"sgt{i}", [128, HT], BF16) for i in range(2)]
        zgt = [sb(f"zgt{i}", [128, HT], BF16) for i in range(2)]
        sut = [sb(f"sut{i}", [128, HT], BF16) for i in range(2)]
        hdnT = sb("hdnT", [128, NJ * HT], BF16)
        tht = [sb(f"tht{i}", [128, HT], BF16) for i in range(2)]
        zt = [sb(f"zt{i}", [128, HT], BF16) for i in range(2)]
        xpc = [sb(f"xpc{i}", [128, HT], F32) for i in range(2)]
        opc = [sb(f"opc{i}", [128, 128], F32) for i in range(8)]
        lntr = sb("lntr", [128, D], BF16)
        stt1 = [sb(f"stt1_{i}", [128, 1], F32) for i in range(2)]
        stt2 = [sb(f"stt2_{i}", [128, 1], F32) for i in range(2)]
        ln_t = [sb(f"lnt{i}", [128, 1], F32) for i in range(6)]

        gw = [bass.AP(wbuf, 0, [[11264, 128], [1, 2048]]),
              bass.AP(wbuf, 2048, [[11264, 128], [1, 2048]])]
        uw = [bass.AP(wbuf, 4096, [[11264, 128], [1, 2048]]),
              bass.AP(wbuf, 6144, [[11264, 128], [1, 2048]])]
        dwb = [bass.AP(wbuf, 0, [[11264, 128], [1, 5632]]),
               bass.AP(wbuf, 5632, [[11264, 128], [1, 5632]])]

        pf = [pst(f"pf{i}", [128, 512], F32) for i in range(6)]
        pb = [pst(f"pb{i}", [128, 1024], BF16) for i in range(2)]

        pl = Plan()
        Q = pl.op
        M = pl.marks

        def dma_group(eng, stream, pairs, waits=()):
            allw = [(stream, pl.n(stream))] + list(waits)
            first = True
            for dst, src in pairs:
                Q(eng, (lambda e, d=dst, s=src: e.dma_start(out=d, in_=s)),
                  waits=tuple(allw) if first else (), inc=stream, amt=16)
                first = False
            return pl.n(stream)

        def blk64(t_, rows, colhalf):
            r = t_[rows * 64:(rows + 1) * 64, :].rearrange("p (b c) -> p b c", b=16)
            return r[:, :, colhalf * 64:(colhalf + 1) * 64]
        def src3(d_):
            return d_.rearrange("p (b c) -> p b c", b=16)
        M[("lnld", 0)] = dma_group("gpsimd", "dx0",
                                   [(xld[0][:], x_d[0:128, :])])
        Q("vector", lambda v: (v.memset(wq[:], 0.0), v.memset(wv[:], 0.0),
                               v.memset(wk[:], 0.0))[0], inc="V")
        VMSW = pl.n("V")
        CW = dma_group("gpsimd", "dk", [
            (blk64(wq, 0, 0), src3(wq_d[0])), (blk64(wq, 1, 1), src3(wq_d[1])),
            (blk64(wv, 0, 0), src3(wv_d[0])), (blk64(wv, 1, 1), src3(wv_d[1])),
            (blk64(wk, 0, 0), src3(wk_d[0])), (blk64(wk, 1, 1), src3(wk_d[1])),
            (wo[0:64, :], wo_d), (wo[64:128, :], wo_d)],
            waits=(("V", VMSW),))
        CX = dma_group("sync", "dc", [
            (bq[:], bq_d), (bk[:], bk_d), (bv[:], bv_d), (bo[:], bo_d),
            (gss[:], gs_d), (uss[:], us_d), (dss[:], ds_d)])
        Q("vector", lambda v: v.memset(ident[:], 0.0), inc="V")
        VMS = pl.n("V")
        Q("gpsimd", lambda g: g.affine_select(
            out=ident[:], in_=ident[:], compare_op=OP.not_equal, fill=1.0,
            base=0, pattern=[[-1, 128]], channel_multiplier=1),
          waits=(("V", VMS),), inc="Gid")
        GID = pl.n("Gid")

        def warmmm(t):
            last = None
            for _ in range(40):
                last = t.matmul(pf[4][:, 0:128], ident[:], ident[:],
                                start=True, stop=True)
            return last
        Q("tensor", warmmm, waits=(("Gid", GID),), inc="T")

        Q("vector", lambda v: v.tensor_scalar(dsh[:], dss[:], 0.5, None, OP.mult),
          waits=(("dc", CX),), inc="V")
        Q("vector", lambda v: v.tensor_scalar(gsh[:], gss[:], 0.5, None, OP.mult),
          inc="V")
        V_DSH = pl.n("V")

        pl.marks[("stage", "consts")] = (len(pl.ops), dict(pl.cnt))
        xn_v = {}
        gu_mm = {}
        gu_sil = {}
        gu_ev = {}
        dn_mm = {}
        dn_th = {}
        fin_v = {}
        st_after = {}
        pb_guard = [0, 0]
        pb_sguard = [0, 0]
        akb = [0, 0]
        akbS = [0, 0]
        out_dmas = 0
        dn_ev = {}
        dn_sread = {}
        dn_txp = {}
        sev_pair = {}
        emit_down_parts = {}
        s_ready_map = {0: 0, 1: 0}
        emit_down_map = {}
        v_attn_end = 0
        t_xnT_read = 0
        t_hdn_read = 0

        def emit_ln_stats(h, i):
            gi = h * NLN + i
            sl = i % 2
            w = []
            if gi >= 2:
                w.append(("V", xn_v.get(gi - 2, 0)))
                w.append(("T", M.get(("lntx", gi - 2), 0)))
            xs = f"dx{sl}"
            if ("lnld", gi) in M:
                ldx = M.pop(("lnld", gi))
            else:
                ldx = dma_group("gpsimd", xs,
                                [(xld[sl][:], x_d[gi * 128:(gi + 1) * 128, :])],
                                waits=tuple(w))
            w = [(xs, ldx)]
            if gi >= 2:
                w.append(("V", xn_v.get(gi - 2, 0)))
            Q("scalar", (lambda s, s_=sl: s.activation(
                lntr[:], xld[s_][:], ACT.Square, accum_out=stt2[s_][:])),
                waits=tuple(w), inc="S")
            Q("scalar", (lambda s, s_=sl: s.activation(
                lntr[:], xld[s_][:], ACT.Identity,
                accum_out=stt1[s_][:])), inc="S")
            sacc = pl.n("S")

            m_, q2_, ve_, rc_, rs_, tp_ = ln_t
            chain = [
                (lambda v, s_=sl: v.tensor_scalar(m_[:], stt1[s_][:], 1.0 / D,
                                                  None, OP.mult)),
                (lambda v, s_=sl: v.tensor_scalar(q2_[:], stt2[s_][:], 1.0 / D,
                                                  None, OP.mult)),
                (lambda v: v.tensor_tensor(tp_[:], m_[:], m_[:], OP.mult)),
                (lambda v: v.tensor_tensor(ve_[:], q2_[:], tp_[:], OP.subtract)),
                (lambda v: v.tensor_scalar(ve_[:], ve_[:], 1.0, EPS,
                                           OP.mult, OP.add)),
                (lambda v: v.tensor_scalar(rs_[:], ve_[:], -0.5, 1.5,
                                           OP.mult, OP.add)),
            ]
            for it in range(3):
                chain += [
                    (lambda v: v.tensor_tensor(tp_[:], rs_[:], rs_[:], OP.mult)),
                    (lambda v: v.tensor_tensor(tp_[:], tp_[:], ve_[:], OP.mult)),
                    (lambda v: v.tensor_scalar(tp_[:], tp_[:], -0.5, 1.5,
                                               OP.mult, OP.add)),
                    (lambda v: v.tensor_tensor(rs_[:], rs_[:], tp_[:], OP.mult)),
                ]
            chain += [
                (lambda v: v.tensor_tensor(tp_[:], m_[:], rs_[:], OP.mult)),
                (lambda v: v.tensor_scalar(tp_[:], tp_[:], -1.0, None, OP.mult)),
                (lambda v, s_=sl: v.tensor_scalar(xld[s_][:], xld[s_][:],
                                                  rs_[:], tp_[:],
                                                  OP.mult, OP.add)),
            ]
            for cf in chain:
                Q("vector", cf,
                  waits=(("S", sacc), ("V", pl.n("V"))), inc="V",
                  keep_self=True)
            xn_v[gi] = pl.n("V")

        def emit_ln_evac(h, i):
            gi = h * NLN + i
            sl = i % 2
            for bat in range(2):
                pbk = pb[bat]
                wt = [("V", xn_v[gi]), ("Gid", GID), ("V", pb_guard[bat]),
                      ("S", pb_sguard[bat])]

                def xput(t, s_=sl, b=bat, pbk_=pbk):
                    last = None
                    for bb in range(8):
                        blk = b * 8 + bb
                        last = t.transpose(pbk_[:, bb * 128:(bb + 1) * 128],
                                           xld[s_][:, blk * 128:(blk + 1) * 128],
                                           ident[:])
                    return last
                Q("tensor", xput, waits=tuple(wt), inc="T")
                tx = pl.n("T")
                M[("lntx", gi)] = tx

                def xev(v, ii=i, b=bat, pbk_=pbk):
                    last = None
                    for bb in range(8):
                        blk = b * 8 + bb
                        last = v.tensor_copy(
                            xnT[:, blk * HT + ii * 128: blk * HT + (ii + 1) * 128],
                            pbk_[:, bb * 128:(bb + 1) * 128])
                    return last
                wv_ = [("T", tx)]
                if h == 1 and i < NLN - 1:
                    wv_.append(("T", gu_mm[(0, NJ - 1)]))
                else:
                    wv_.append(("T", t_xnT_read))
                Q("vector", xev, waits=tuple(wv_), inc="V")
                pb_guard[bat] = pl.n("V")
            M[("xnTt", h, i)] = pl.n("V")
            pl.marks[("stage", f"lnt{gi}")] = (len(pl.ops), dict(pl.cnt))

        for h in range(2):
            if h == 0:
                emit_ln_stats(0, 0)
                emit_ln_evac(0, 0)
            pl.marks[("stage", f"ln{h}")] = (len(pl.ops), dict(pl.cnt))

            if h == 0:
                Q("scalar", lambda s: s.preload_act_table(ACT.Exp),
                  waits=(("S", pl.n("S")),))

            def emit_gu(j):
                sl = j % 2
                w = []
                if j >= 2:
                    w.append(("T", gu_mm[(h, j - 2)]))
                elif h == 1:
                    w.append(("T", dn_mm[(0, NHH - 1)]))
                gs_ = f"dg{sl}"
                wld = dma_group("gpsimd", gs_,
                                [(gw[sl], gwb_d[j]), (uw[sl], uwb_d[j])],
                                waits=tuple(w))
                pg, pu = pf[0 + sl * 2], pf[1 + sl * 2]
                w = [(gs_, wld), ("V", M[("xnTt", h, NLN - 1)])]
                if j >= 2:
                    w.append(("V", gu_ev[(h, j - 2)]))
                    w.append(("S", gu_sil[(h, j - 2)]))
                elif h == 1:
                    w.append(("V", dn_ev[(0, NHH - 1)]))
                    w.append(("S", dn_th[(0, NHH - 1)]))

                def gumm(t, s_=sl, pg_=pg, pu_=pu):
                    for k in range(NHH):
                        t.matmul(pg_[:], gw[s_][:, k * 128:(k + 1) * 128],
                                 xnT[:, k * HT:(k + 1) * HT],
                                 start=(k == 0), stop=(k == NHH - 1))
                    last = None
                    for k in range(NHH):
                        last = t.matmul(pu_[:], uw[s_][:, k * 128:(k + 1) * 128],
                                        xnT[:, k * HT:(k + 1) * HT],
                                        start=(k == 0), stop=(k == NHH - 1))
                    return last
                Q("tensor", gumm, waits=tuple(w), inc="T")
                gu_mm[(h, j)] = pl.n("T")
                Q("scalar", (lambda s, jj=j, s_=sl, pg_=pg: s.activation(
                    sgt[s_][:], pg_[:], ACT.Tanh, scale=gsh[:, jj:jj + 1])),
                    waits=(("T", gu_mm[(h, j)]), ("V", V_DSH)), inc="S")
                Q("scalar", (lambda s, jj=j, s_=sl, pg_=pg: s.activation(
                    zgt[s_][:], pg_[:], ACT.Identity, scale=gsh[:, jj:jj + 1])),
                    inc="S")
                Q("scalar", (lambda s, jj=j, s_=sl, pu_=pu: s.activation(
                    sut[s_][:], pu_[:], ACT.Identity, scale=uss[:, jj:jj + 1])),
                    inc="S")
                gu_sil[(h, j)] = pl.n("S")
                whd = [("S", gu_sil[(h, j)])]
                if j == 0 and h == 1:
                    whd.append(("T", t_hdn_read))
                Q("vector", (lambda v, s_=sl: v.scalar_tensor_tensor(
                    zgt[s_][:], sgt[s_][:], 1.0, zgt[s_][:], OP.add, OP.mult)),
                    waits=tuple(whd), inc="V")
                Q("vector", (lambda v, jj=j, s_=sl: v.tensor_tensor(
                    hdnT[:, jj * HT:(jj + 1) * HT], zgt[s_][:], sut[s_][:], OP.mult)),
                    inc="V")
                gu_ev[(h, j)] = pl.n("V")

            def emit_attn(c, filler=None, per=8, per_z=0, skip_k=False,
                          next_k=None):
                it = attn_stepper(c, skip_k=skip_k, next_k=next_k)
                emitted = 0
                for i, tag in enumerate(it):
                    if filler is None:
                        continue
                    if tag[0] == "P":
                        want = ((tag[1] + 1) * per) // NPAIR
                        filler(want - emitted)
                        emitted = want
                    elif tag[0] == "z":
                        filler(per_z)

            def emit_kphase(hh, c):
                tc = c * CK
                for bat in range(8):
                    b = bat % 2
                    ps_ = pf[4 + b]
                    w = [("V", akb[b]), ("S", akbS[b]), ("V", M[("xnTt", hh, c)]),
                         ("dk", CW)]

                    def kmm(t, bt=bat, ps=ps_, tc_=tc):
                        last = None
                        for q4 in range(4):
                            e = bt * 4 + q4
                            jj = e // 2
                            lhs = wk[:, e * 64:(e + 1) * 64]
                            rhs = xnT[:, jj * HT + tc_: jj * HT + tc_ + CK]
                            t.matmul(ps[0:64, q4 * 128:(q4 + 1) * 128], lhs, rhs,
                                     start=True, stop=True)
                            last = t.matmul(ps[64:128, q4 * 128:(q4 + 1) * 128],
                                            lhs, rhs, start=True, stop=True)
                        return last
                    Q("tensor", kmm, waits=tuple(w), inc="T")
                    tk = pl.n("T")

                    def kev(s, bt=bat, ps=ps_):
                        last = None
                        for q4 in range(4):
                            e = bt * 4 + q4
                            last = s.activation(
                                kdup[:, e * 128:(e + 1) * 128],
                                ps[:, q4 * 128:(q4 + 1) * 128],
                                ACT.Identity, bias=bk[:, e:e + 1])
                        return last
                    Q("scalar", kev,
                      waits=(("T", tk), ("V", M.get(("vs_last",), 0))), inc="S")
                    akbS[b] = pl.n("S")
                M[("kready", hh, c)] = pl.n("S")

            def attn_stepper(c, womm_cb=None, skip_k=False, next_k=None):
                tc = c * CK
                if not skip_k:
                    emit_kphase(h, c)
                kdup_ready_s = M[("kready", h, c)]
                yield ("k", c)

                def do_qse(P):
                    sl = P % 2
                    w = [("V", akb[0]), ("V", akb[1]), ("S", akbS[0]), ("S", akbS[1])]

                    def qvmm(t, Pp=P, tc_=tc):
                        rhs = xnT[:, Pp * HT + tc_: Pp * HT + tc_ + CK]
                        t.matmul(pf[4][0:64, 0:128],
                                 wq[:, Pp * 128: Pp * 128 + 64], rhs,
                                 start=True, stop=True)
                        t.matmul(pf[4][64:128, 0:128],
                                 wq[:, Pp * 128 + 64:(Pp + 1) * 128], rhs,
                                 start=True, stop=True)
                        t.matmul(pf[5][0:64, 0:128],
                                 wv[:, Pp * 128: Pp * 128 + 64], rhs,
                                 start=True, stop=True)
                        return t.matmul(pf[5][64:128, 0:128],
                                        wv[:, Pp * 128 + 64:(Pp + 1) * 128], rhs,
                                        start=True, stop=True)
                    Q("tensor", qvmm, waits=tuple(w), inc="T")
                    tqv = pl.n("T")

                    def qvev(s, Pp=P, s_=sl):
                        s.activation(qp[s_][:], pf[4][:, 0:128], ACT.Identity,
                                     bias=bq[:, Pp:Pp + 1])
                        return s.activation(vpb[s_][:], pf[5][:, 0:128],
                                            ACT.Identity, bias=bv[:, Pp:Pp + 1])
                    Q("scalar", qvev, waits=(("T", tqv),), inc="S")
                    akbS[0] = akbS[1] = pl.n("S")
                    qvdone_s = pl.n("S")

                    w = [("S", qvdone_s), ("S", kdup_ready_s),
                         ("S", M.get(("exps", 0), 0))]

                    def smul(v, s_=sl):
                        qrep = bass.AP(qp[s_], 0, [[CK, 128], [0, 32], [1, CK]])
                        return v.tensor_tensor(
                            sping[:].rearrange("p (r n) -> p r n", r=32),
                            qrep,
                            kdup[:].rearrange("p (r n) -> p r n", r=32),
                            OP.mult)
                    Q("vector", smul, waits=tuple(w), inc="V")
                    vs = pl.n("V")
                    M[("vs_last",)] = vs
                    Q("scalar", (lambda s, s_=sl: s.activation(
                        Et[s_][:], sping[:], ACT.Exp, scale=0.125)),
                        waits=(("V", vs), ("V", M.get(("uctxp", sl), 0))),
                        inc="S")
                    M[("exps", 0)] = pl.n("S")
                    M[("expdone", P % 2)] = pl.n("S")

                def do_zu(P):
                    sl = P % 2

                    def ztree(v, s_=sl):
                        ev = bass.AP(Et[s_], 0, [[NE * CK, 128], [256, 16], [1, 128]])
                        od = bass.AP(Et[s_], 128, [[NE * CK, 128], [256, 16], [1, 128]])
                        v.tensor_tensor(
                            zscr[:, 0:2048].rearrange("p (r n) -> p r n", r=16),
                            ev, od, OP.add)
                        e2 = bass.AP(zscr, 0, [[2048, 128], [256, 8], [1, 128]])
                        o2 = bass.AP(zscr, 128, [[2048, 128], [256, 8], [1, 128]])
                        v.tensor_tensor(
                            zscr[:, 0:1024].rearrange("p (r n) -> p r n", r=8),
                            e2, o2, OP.add)
                        e3 = bass.AP(zscr, 0, [[2048, 128], [256, 4], [1, 128]])
                        o3 = bass.AP(zscr, 128, [[2048, 128], [256, 4], [1, 128]])
                        v.tensor_tensor(
                            zscr[:, 0:512].rearrange("p (r n) -> p r n", r=4),
                            e3, o3, OP.add)
                        e4 = bass.AP(zscr, 0, [[2048, 128], [256, 2], [1, 128]])
                        o4 = bass.AP(zscr, 128, [[2048, 128], [256, 2], [1, 128]])
                        v.tensor_tensor(
                            zscr[:, 0:256].rearrange("p (r n) -> p r n", r=2),
                            e4, o4, OP.add)
                        v.tensor_tensor(Zf[:], zscr[:, 0:128], zscr[:, 128:256],
                                        OP.add)
                        v.reciprocal(Zr[:], Zf[:])
                        return v.tensor_tensor(wP[s_][:], vpb[s_][:], Zr[:], OP.mult)
                    Q("vector", ztree, waits=(("S", M[("expdone", sl)]),), inc="V")
                    wready = pl.n("V")

                    def uctx(v, Pp=P, s_=sl):
                        last = None
                        for mg in range(2):
                            wrep = bass.AP(wP[s_], 0, [[CK, 128], [0, 16], [1, CK]])
                            esl = Et[s_][:, mg * 2048:(mg + 1) * 2048].rearrange(
                                "p (r n) -> p r n", r=16)
                            csl = ctxa[:, mg * 2048:(mg + 1) * 2048].rearrange(
                                "p (r n) -> p r n", r=16)
                            if Pp == 0:
                                last = v.tensor_tensor(csl, wrep, esl, OP.mult)
                            else:
                                ut = utmp[:].rearrange("p (r n) -> p r n", r=16)
                                v.tensor_tensor(ut, wrep, esl, OP.mult)
                                last = v.tensor_tensor(csl, csl, ut, OP.add)
                        return last
                    Q("vector", uctx, waits=(("V", wready),), inc="V")
                    M[("uctxp", sl)] = pl.n("V")

                do_qse(0)
                yield ("P", 0)
                for P in range(1, NPAIR):
                    do_qse(P)
                    if P == NPAIR - 1 and next_k is not None:
                        emit_kphase(*next_k)
                    do_zu(P - 1)
                    yield ("P", P)
                do_zu(NPAIR - 1)
                pctx = pl.n("V")
                yield ("z", c)

                for grp in range(4):
                    b = grp % 2
                    ps_ = pf[4 + b]
                    jps = list(range(grp * 4, grp * 4 + 4))
                    w = [("V", pctx), ("S", akbS[b]), ("V", akb[b])]

                    def womm(t, jj=tuple(jps), ps=ps_):
                        last = None
                        for q, j_ in enumerate(jj):
                            me, mo = 2 * j_, 2 * j_ + 1
                            t.matmul(ps[0:64, q * 128:(q + 1) * 128], wo[:, 0:64],
                                     ctxa[:, me * 128:(me + 1) * 128],
                                     start=True, stop=True)
                            last = t.matmul(ps[64:128, q * 128:(q + 1) * 128],
                                            wo[:, 0:64],
                                            ctxa[:, mo * 128:(mo + 1) * 128],
                                            start=True, stop=True)
                        return last
                    Q("tensor", womm, waits=tuple(w), inc="T")
                    tw = pl.n("T")

                    wsev = [("T", tw)] + [("V", dn_sread.get((h - 1, j_), 0))
                                          for j_ in jps]

                    def sev(s, jj=tuple(jps), tc_=tc, ps=ps_):
                        last = None
                        for q, j_ in enumerate(jj):
                            last = s.activation(
                                s_sb[:, j_ * HT + tc_: j_ * HT + tc_ + CK],
                                ps[:, q * 128:(q + 1) * 128], ACT.Identity,
                                bias=bo[:])
                        return last
                    Q("scalar", sev, waits=tuple(wsev), inc="S")
                    akbS[b] = pl.n("S")
                    akb[b] = max(akb[b], pctx)
                    for j_ in jps:
                        sev_pair[(h, j_)] = pl.n("S")
                    if womm_cb is not None:
                        for j_ in jps:
                            womm_cb(j_)
                s_ready_map[h] = pl.n("S")
                yield ("end", c)

            def emit_xpc_load(hd, hh, eng="sync", pfx="dp"):
                ghh = hd * NHH + hh
                sl = hh % 2
                wx = []
                if ghh >= 2:
                    wx.append(("V", fin_v.get(ghh - 2, 0)))
                ps_str = f"{pfx}{sl}"
                xl = dma_group(eng, ps_str, [(
                    xpc[sl][:].rearrange("p (a b) -> p a b", a=4),
                    bass.AP(x_h, (hd * HT) * D + hh * 128,
                            [[D, 128], [128 * D, 4], [1, 128]]))],
                    waits=tuple(wx))
                M[("dnxl", ghh)] = (ps_str, xl)

            def emit_down_mm(hd, hh, xpc_at_mm=True):
                ghh = hd * NHH + hh
                sl = hh % 2
                w = [("T", gu_mm[(hd, NJ - 1)])]
                if hh >= 2:
                    w.append(("T", dn_mm[(hd, hh - 2)]))
                ds_ = f"dd{sl}"
                dld = dma_group("gpsimd", ds_, [(dwb[sl], dwb_d[hh])],
                                waits=tuple(w))
                if xpc_at_mm:
                    emit_xpc_load(hd, hh)
                pd = pf[hh % 4]
                w = [(ds_, dld), ("V", gu_ev[(hd, NJ - 1)])]
                if hh >= 4:
                    w.append(("S", dn_th[(hd, hh - 4)]))
                    w.append(("V", dn_ev[(hd, hh - 4)]))
                else:
                    w.append(("S", gu_sil[(hd, NJ - 1)]))
                    w.append(("V", gu_ev[(hd, NJ - 1)]))

                def dnmm(t, s_=sl, pd_=pd):
                    last = None
                    for j in range(NJ):
                        last = t.matmul(pd_[:], dwb[s_][:, j * 128:(j + 1) * 128],
                                        hdnT[:, j * HT:(j + 1) * HT],
                                        start=(j == 0), stop=(j == NJ - 1))
                    return last
                Q("tensor", dnmm, waits=tuple(w), inc="T")
                dn_mm[(hd, hh)] = pl.n("T")

            def emit_down_evac(hd, hh, zslot=None, fin=True):
                sl = hh % 2
                pd = pf[hh % 4]
                zs = zslot if zslot is not None else zt[sl][:]
                Q("scalar", (lambda s, k=hh, s_=sl, pd_=pd: s.activation(
                    tht[s_][:], pd_[:], ACT.Tanh, scale=dsh[:, k:k + 1])),
                    waits=(("T", dn_mm[(hd, hh)]), ("V", V_DSH)), inc="S")
                dn_th[(hd, hh)] = pl.n("S")
                wv_ = [("S", dn_th[(hd, hh)])]
                if hd == 1 and zslot is not None:
                    wv_.append(("T", M.get(("lntx", 2 * NLN - 1), 0)))

                def dnev(v, k=hh, s_=sl, pd_=pd, zs_=zs):
                    v.tensor_scalar(zs_, pd_[:], dsh[:, k:k + 1], None, OP.mult)
                    return v.scalar_tensor_tensor(zs_, tht[s_][:], 1.0, zs_,
                                                  OP.add, OP.mult)
                Q("vector", dnev, waits=tuple(wv_), inc="V")
                dn_ev[(hd, hh)] = pl.n("V")
                if fin:
                    emit_down_fin(hd, hh, zslot=zs)

            def emit_down_fin(hd, hh, zslot=None):
                nonlocal out_dmas
                ghh = hd * NHH + hh
                sl = hh % 2
                zs = zslot if zslot is not None else zt[sl][:]
                if ("dnxl", ghh) not in M:
                    emit_xpc_load(hd, hh, eng="gpsimd", pfx="dx")
                ps_str, xl = M[("dnxl", ghh)]

                def dfin(v, k=hh, s_=sl, zs_=zs):
                    return v.tensor_tensor(tht[s_][:], zs_,
                                           s_sb[:, k * HT:(k + 1) * HT], OP.mult)
                Q("vector", dfin, waits=(("V", dn_ev[(hd, hh)]),
                                         ("S", sev_pair.get((hd, hh), 0)),
                                         ("T", dn_txp.get((hd, hh - 2), 0))),
                  inc="V")
                vP_ = pl.n("V")
                dn_sread[(hd, hh)] = vP_
                pbk = pb[hh % 2]
                wt = [("V", vP_), ("Gid", GID), ("V", pb_guard[hh % 2]),
                      ("S", pb_sguard[hh % 2])]

                def pxp(t, s_=sl, pbk_=pbk):
                    last = None
                    for tb in range(4):
                        last = t.transpose(pbk_[:, tb * 128:(tb + 1) * 128],
                                           tht[s_][:, tb * 128:(tb + 1) * 128],
                                           ident[:])
                    return last
                Q("tensor", pxp, waits=tuple(wt), inc="T")
                txp = pl.n("T")
                dn_txp[(hd, hh)] = txp

                def finadd(v, s_=sl, pbk_=pbk, par=(hh % 2)):
                    last = None
                    for tb in range(4):
                        last = v.tensor_tensor(opc[par * 4 + tb][:],
                                               pbk_[:, tb * 128:(tb + 1) * 128],
                                               xpc[s_][:, tb * 128:(tb + 1) * 128],
                                               OP.add)
                    return last
                Q("vector", finadd,
                  waits=(("T", txp), (ps_str, xl),
                         (f"st{hh % 2}", st_after.get(ghh - 2, 0))), inc="V")
                fin_v[ghh] = pl.n("V")
                pb_guard[hh % 2] = fin_v[ghh]
                ss_ = f"st{hh % 2}"
                dma_group("sync", ss_, [(
                    bass.AP(out_h, (hd * HT + tb * 128) * D + hh * 128,
                            [[D, 128], [1, 128]]),
                    opc[(hh % 2) * 4 + tb][:]) for tb in range(4)],
                    waits=(("V", fin_v[ghh]),))
                out_dmas += 4
                st_after[ghh] = pl.n(ss_)

            def emit_down_hh(hd, hh):
                emit_down_mm(hd, hh)
                emit_down_evac(hd, hh)

            emit_down_map[h] = emit_down_hh
            emit_down_parts[h] = (emit_down_mm, emit_down_evac)

            units = []
            if h == 0:
                units += [(lambda ii=i: (emit_ln_stats(0, ii),
                                         emit_ln_evac(0, ii)))
                          for i in (1, 2, 3)]
            units += [(lambda jj=j: emit_gu(jj)) for j in range(NJ)]
            if h == 0:
                for i in range(NLN):
                    units.append(lambda ii=i: emit_ln_stats(1, ii))
                    if i < NLN - 1:
                        units.append(lambda ii=i: emit_ln_evac(1, ii))
            uq = iter(units)
            nu = len(units)

            def drain(n):
                for _ in range(n):
                    u = next(uq, None)
                    if u is not None:
                        u()
            if h == 0:
                for c, per in zip(range(NCH), (11, 13, 13, 17)):
                    emit_attn(c, filler=drain, per=per)
                drain(nu)
            else:
                g = attn_stepper(0)
                edh = emit_down_map[0]
                dn0 = iter(range(NHH))
                for tag in g:
                    if tag[0] == "k":
                        emit_ln_evac(1, NLN - 1)
                    elif tag[0] == "P":
                        hh = next(dn0, None)
                        if hh is not None:
                            edh(0, hh)
                t_hdn_read = pl.n("T")
                emit_attn(1, filler=drain, per=22)
                emit_attn(2, filler=drain, per=22)
                drain(nu)
                dn1_zsl = ([xld[0][:, q * HT:(q + 1) * HT] for q in range(4)]
                           + [xld[1][:, q * HT:(q + 1) * HT] for q in range(4)]
                           + [sgt[0][:], sgt[1][:], zgt[0][:], zgt[1][:],
                              sut[0][:], sut[1][:], zt[0][:], zt[1][:]])
                dn1 = iter(range(NHH))
                dn1_pend = []

                def dn1_step():
                    if dn1_pend:
                        hhp = dn1_pend.pop(0)
                        emit_down_evac(1, hhp, zslot=dn1_zsl[hhp], fin=False)
                    hh2 = next(dn1, None)
                    if hh2 is not None:
                        emit_down_mm(1, hh2, xpc_at_mm=False)
                        dn1_pend.append(hh2)
                for tag in attn_stepper(3):
                    if tag[0] == "P":
                        dn1_step()
                while True:
                    if dn1_pend:
                        hh2 = dn1_pend.pop(0)
                        emit_down_evac(1, hh2, zslot=dn1_zsl[hh2], fin=False)
                        continue
                    hh2 = next(dn1, None)
                    if hh2 is None:
                        break
                    emit_down_mm(1, hh2, xpc_at_mm=False)
                    dn1_pend.append(hh2)
                for hh2 in range(NHH):
                    emit_down_fin(1, hh2, zslot=dn1_zsl[hh2])
            v_attn_end = pl.n("V")
            t_xnT_read = pl.n("T")

            pl.marks[("stage", f"attn{h}")] = (len(pl.ops), dict(pl.cnt))


            pl.marks[("stage", f"down{h}")] = (len(pl.ops), dict(pl.cnt))

        global LAST_MARKS
        LAST_MARKS = {k2[1]: v[0] for k2, v in pl.marks.items()
                      if isinstance(k2, tuple) and k2[0] == "stage"}
        if stage is not None:
            if isinstance(stage, str) and stage.startswith("op"):
                n_cut = int(stage[2:])
                cc = {}
                for (e_, w_, f_, i_, a_) in pl.ops[:n_cut]:
                    if i_ is not None:
                        cc[i_] = cc.get(i_, 0) + a_
                pl.marks[("stage", stage)] = (n_cut, cc)
            nops, cnts = pl.marks[("stage", stage)]
            pl.ops = pl.ops[:nops]
            dma_owner = {"dc": "sync", "dp0": "sync", "dp1": "sync",
                         "st0": "sync", "st1": "sync",
                         "dk": "gpsimd", "dx0": "gpsimd", "dx1": "gpsimd",
                         "dg0": "gpsimd", "dg1": "gpsimd",
                         "dd0": "gpsimd", "dd1": "gpsimd"}
            for sname, eng in dma_owner.items():
                v = cnts.get(sname, 0)
                if v > 0:
                    pl.ops.append((eng, [(sname, v)], (lambda e: None), None, 1))
            dumps = [("sb_xld0", xld[0]), ("sb_xld1", xld[1]),
                     ("sb_lnt0", ln_t[0]), ("sb_lnt1", ln_t[1]), ("sb_lnt2", ln_t[2]), ("sb_lnt3", ln_t[3]), ("sb_lnt4", ln_t[4]), ("sb_lnt5", ln_t[5]), ("sb_stt1", stt1[0]), ("sb_stt2", stt2[0]),
                     ("sb_wq", wq), ("sb_wk", wk), ("sb_wv", wv), ("sb_xnT", xnT)]
            if not (stage.startswith("ln") or stage.startswith("consts")):
                dumps += [
                     ("sb_kdup", kdup), ("sb_qp0", qp[0]), ("sb_vp0", vpb[0]),
                     ("sb_sping", sping), ("sb_Et", Et), ("sb_Zf", Zf),
                     ("sb_ctxa", ctxa), ("sb_s_sb", s_sb), ("sb_hdnT", hdnT)]
            cw = [("T", cnts.get("T", 0)), ("V", cnts.get("V", 0)),
                  ("S", cnts.get("S", 0))]
            col = 0
            pairs = []
            dump_cols = {}
            for nm, t_ in dumps:
                ncol = t_.shape[1] if not isinstance(t_, bass.AP) else t_.shape[1]
                ncol = min(ncol, 8192)
                dump_cols[nm] = (col, ncol)
                pairs.append((bass.AP(dbg_h, col, [[65536, 128], [1, ncol]]),
                              t_[:, 0:ncol]))
                col += ncol
            global DUMP_COLS
            DUMP_COLS = dump_cols
            for idx, (d_, s_) in enumerate(pairs):
                pl.ops.append(("gpsimd", cw if idx == 0 else [], (
                    lambda e, dd=d_, ss=s_: e.dma_start(out=dd, in_=ss, allow_slow_non_contiguous=True)), "st0", 16))
            pl.ops.append(("gpsimd", [("st0", cnts.get("st0", 0) + 16 * len(pairs))],
                           (lambda e: None), None, 1))
        else:
            Q("sync", lambda e: None,
              waits=(("st0", pl.n("st0")), ("st1", pl.n("st1"))))

        block = ctx.enter_context(nc.Block())

        def runner(engname):
            def body(eng):
                for (e, waits, emit, inc, amt) in pl.ops:
                    if e != engname:
                        continue
                    for (s, v) in waits:
                        eng.wait_ge(semh[s], v)
                    inst = emit(eng)
                    if inc is not None:
                        assert inst is not None, f"{engname} op with inc returned None"
                        inst.then_inc(semh[inc], amt)
            return body

        block.sync(runner("sync"))
        block.gpsimd(runner("gpsimd"))
        block.tensor(runner("tensor"))
        block.vector(runner("vector"))
        block.scalar(runner("scalar"))

    return nc


LAST_MARKS = {}
DUMP_COLS = {}
_NC_CACHE = None


def _get_nc():
    global _NC_CACHE
    if _NC_CACHE is None:
        _NC_CACHE = build_program()
    return _NC_CACHE


def kernel(L0_attn_x, L0_attn_Wqkv, L0_attn_bqkv, L0_attn_Wout, L0_attn_bout,
           L0_ffn_gate_w, L0_ffn_up_w, L0_ffn_down_w, L0_ffn_gate_s, L0_ffn_up_s,
           L0_ffn_down_s):
    asc = np.ascontiguousarray
    f32 = np.float32
    x = asc(np.asarray(L0_attn_x, f32).reshape(TOK * NCORES, D))
    Wqkv = np.asarray(L0_attn_Wqkv, f32)
    bqkv = np.asarray(L0_attn_bqkv, f32)
    Wout = asc(np.asarray(L0_attn_Wout, f32))
    bout = np.asarray(L0_attn_bout, f32)
    gate_w = np.asarray(L0_ffn_gate_w, f32)
    up_w = np.asarray(L0_ffn_up_w, f32)
    down_w = np.asarray(L0_ffn_down_w, f32)
    gate_s = np.asarray(L0_ffn_gate_s, f32)
    up_s = np.asarray(L0_ffn_up_s, f32)
    down_s = np.asarray(L0_ffn_down_s, f32)

    gate_blk = asc(gate_w.T.reshape(NHH, 128, NJ, 128).transpose(2, 1, 0, 3)
                   ).reshape(NJ, 128, NHH * 128)
    up_blk = asc(up_w.T.reshape(NHH, 128, NJ, 128).transpose(2, 1, 0, 3)
                 ).reshape(NJ, 128, NHH * 128)
    down_blk = asc(down_w.T.reshape(NJ, 128, NHH, 128).transpose(2, 1, 0, 3)
                   ).reshape(NHH, 128, NJ * 128)
    wq = asc(np.stack([Wqkv[0::2, :, 0:64].transpose(1, 0, 2).reshape(64, NPAIR * 64),
                       Wqkv[1::2, :, 0:64].transpose(1, 0, 2).reshape(64, NPAIR * 64)]))
    wv = asc(np.stack([Wqkv[0::2, :, 128:192].transpose(1, 0, 2).reshape(64, NPAIR * 64),
                       Wqkv[1::2, :, 128:192].transpose(1, 0, 2).reshape(64, NPAIR * 64)]))
    wk = asc(np.stack([Wqkv[0::2, :, 64:128].transpose(1, 0, 2).reshape(64, NPAIR * 64),
                       Wqkv[1::2, :, 64:128].transpose(1, 0, 2).reshape(64, NPAIR * 64)]))
    bq = asc(bqkv[:, 0:64].reshape(NPAIR, 2, 64).transpose(1, 2, 0).reshape(128, NPAIR))
    bkh = asc(np.concatenate([bqkv[:, 64:128].T, bqkv[:, 64:128].T], axis=0))
    bv = asc(bqkv[:, 128:192].reshape(NPAIR, 2, 64).transpose(1, 2, 0).reshape(128, NPAIR))
    bo = asc(np.concatenate([bout, bout])[:, None])
    gs = asc(gate_s.reshape(NJ, 128).T)
    us = asc(up_s.reshape(NJ, 128).T)
    ds = asc(down_s.reshape(NHH, 128).T)

    nc = _get_nc()
    shared = {
        "gate_blk": gate_blk, "up_blk": up_blk, "down_blk": down_blk,
        "wq": wq, "wk": wk, "wv": wv, "wo": Wout,
        "bq": bq, "bk": bkh, "bv": bv, "bo": bo,
        "gs": gs, "us": us, "ds": ds,
    }
    in_maps = []
    for c in range(NCORES):
        m = dict(shared)
        m["x"] = asc(x[c * TOK:(c + 1) * TOK])
        in_maps.append(m)
    res = run_bass_kernel_spmd(nc, in_maps, core_ids=list(range(NCORES)))
    out = np.concatenate([np.asarray(r["out"]) for r in res.results], axis=0)
    return out.reshape(4, 2048, 2048).astype(np.float32)


if __name__ == "__main__":
    print("building program...")
    nc = build_program()
    print("built ok")


# version 25
# speedup vs baseline: 1.0051x; 1.0026x over previous
import numpy as np
import concourse.bass as bass
import concourse.mybir as mybir
from concourse.bass_utils import run_bass_kernel_spmd
from contextlib import ExitStack

F32 = mybir.dt.float32
BF16 = mybir.dt.bfloat16
OP = mybir.AluOpType
ACT = mybir.ActivationFunctionType

NCORES = 8
TOK = 1024
D = 2048
HT = 512
NLN = 4
NJ = 44
NHH = 16
NPAIR = 16
NE = 32
CK = 128
NCH = HT // CK
EPS = 1e-5


class Plan:

    def __init__(self):
        self.ops = []
        self.cnt = {}
        self.waited = {}
        self.marks = {}

    def n(self, sem):
        return self.cnt.get(sem, 0)

    def op(self, eng, emit, waits=(), inc=None, amt=1, keep_self=False):
        pruned = []
        for (s, v) in waits:
            if v <= 0:
                continue
            assert v <= self.cnt.get(s, 0), f"forward wait {s}={v}>{self.cnt.get(s, 0)}"
            if _OWNER.get(s) == eng and not keep_self:
                continue
            k = (eng, s)
            if self.waited.get(k, 0) >= v:
                continue
            self.waited[k] = v
            pruned.append((s, v))
        if inc is not None:
            self.cnt[inc] = self.cnt.get(inc, 0) + amt
        self.ops.append((eng, pruned, emit, inc, amt))
        return self.cnt.get(inc, 0) if inc else None


_OWNER = {"T": "tensor", "V": "vector", "S": "scalar", "G": "gpsimd",
          "Gid": "gpsimd"}


def build_program(stage=None):
    nc = bass.Bass("TRN2", target_bir_lowering=False, detect_race_conditions=False)

    x_h = nc.dram_tensor("x", [TOK, D], F32, kind="ExternalInput")
    out_h = nc.dram_tensor("out", [TOK, D], F32, kind="ExternalOutput")
    dbg_h = (nc.dram_tensor("dbg", [128, 65536], F32, kind="ExternalOutput")
             if stage is not None else None)
    x_d = x_h.ap()
    gwb_d = nc.dram_tensor("gate_blk", [NJ, 128, NHH * 128], F32, kind="ExternalInput").ap()
    uwb_d = nc.dram_tensor("up_blk", [NJ, 128, NHH * 128], F32, kind="ExternalInput").ap()
    dwb_d = nc.dram_tensor("down_blk", [NHH, 128, NJ * 128], F32, kind="ExternalInput").ap()
    wq_d = nc.dram_tensor("wq", [2, 64, NPAIR * 64], F32, kind="ExternalInput").ap()
    wk_d = nc.dram_tensor("wk", [2, 64, NPAIR * 64], F32, kind="ExternalInput").ap()
    wv_d = nc.dram_tensor("wv", [2, 64, NPAIR * 64], F32, kind="ExternalInput").ap()
    wo_d = nc.dram_tensor("wo", [64, 64], F32, kind="ExternalInput").ap()
    bq_d = nc.dram_tensor("bq", [128, NPAIR], F32, kind="ExternalInput").ap()
    bk_d = nc.dram_tensor("bk", [128, NE], F32, kind="ExternalInput").ap()
    bv_d = nc.dram_tensor("bv", [128, NPAIR], F32, kind="ExternalInput").ap()
    bo_d = nc.dram_tensor("bo", [128, 1], F32, kind="ExternalInput").ap()
    gs_d = nc.dram_tensor("gs", [128, NJ], F32, kind="ExternalInput").ap()
    us_d = nc.dram_tensor("us", [128, NJ], F32, kind="ExternalInput").ap()
    ds_d = nc.dram_tensor("ds", [128, NHH], F32, kind="ExternalInput").ap()

    with ExitStack() as ctx:
        sb = lambda n_, s, d: ctx.enter_context(nc.sbuf_tensor("sb_" + n_, s, d))
        pst = lambda n_, s, d: ctx.enter_context(nc.psum_tensor("ps_" + n_, s, d))
        semh = {name: ctx.enter_context(nc.semaphore(name))
                for name in ("dc", "dk", "dx0", "dx1", "dg0", "dg1", "dd0", "dd1",
                             "dp0", "dp1", "st0", "st1", "T", "V", "S", "G", "Gid")}

        wq = sb("wq", [128, NPAIR * 128], BF16)
        wv = sb("wv", [128, NPAIR * 128], BF16)
        wk = sb("wk", [128, NE * 64], BF16)
        wo = sb("wo", [128, 64], BF16)
        bq = sb("bq", [128, NPAIR], F32)
        bk = sb("bk", [128, NE], F32)
        bv = sb("bv", [128, NPAIR], F32)
        bo = sb("bo", [128, 1], F32)
        gss = sb("gss", [128, NJ], F32)
        uss = sb("uss", [128, NJ], F32)
        dss = sb("dss", [128, NHH], F32)
        dsh = sb("dsh", [128, NHH], F32)
        gsh = sb("gsh", [128, NJ], F32)
        ident = sb("ident", [128, 128], BF16)
        xld = [sb(f"xld{i}", [128, D], BF16) for i in range(2)]
        xnT = sb("xnT", [128, NHH * HT], BF16)
        kdup = sb("kdup", [128, NE * CK], BF16)
        qp = [sb(f"qp{i}", [128, CK], BF16) for i in range(2)]
        vpb = [sb(f"vp{i}", [128, CK], BF16) for i in range(2)]
        sping = sb("sping", [128, 4096], BF16)
        Et = [sb(f"Et{i}", [128, NE * CK], BF16) for i in range(2)]
        zscr = sb("zscr", [128, 2048], BF16)
        Zf = sb("Zf", [128, CK], F32)
        Zr = sb("Zr", [128, CK], F32)
        wP = [sb(f"wP{i}", [128, CK], BF16) for i in range(2)]
        utmp = sb("utmp", [128, 2048], BF16)
        ctxa = sb("ctxa", [128, NE * CK], BF16)
        s_sb = sb("s_sb", [128, NPAIR * HT], BF16)
        wbuf = sb("wbuf", [128, 11264], BF16)
        sgt = [sb(f"sgt{i}", [128, HT], BF16) for i in range(2)]
        zgt = [sb(f"zgt{i}", [128, HT], BF16) for i in range(2)]
        sut = [sb(f"sut{i}", [128, HT], BF16) for i in range(2)]
        hdnT = sb("hdnT", [128, NJ * HT], BF16)
        tht = [sb(f"tht{i}", [128, HT], BF16) for i in range(2)]
        zt = [sb(f"zt{i}", [128, HT], BF16) for i in range(2)]
        xpc = [sb(f"xpc{i}", [128, HT], F32) for i in range(2)]
        opc = [sb(f"opc{i}", [128, 128], F32) for i in range(8)]
        lntr = sb("lntr", [128, D], BF16)
        stt1 = [sb(f"stt1_{i}", [128, 1], F32) for i in range(2)]
        stt2 = [sb(f"stt2_{i}", [128, 1], F32) for i in range(2)]
        ln_t = [sb(f"lnt{i}", [128, 1], F32) for i in range(6)]

        gw = [bass.AP(wbuf, 0, [[11264, 128], [1, 2048]]),
              bass.AP(wbuf, 2048, [[11264, 128], [1, 2048]])]
        uw = [bass.AP(wbuf, 4096, [[11264, 128], [1, 2048]]),
              bass.AP(wbuf, 6144, [[11264, 128], [1, 2048]])]
        dwb = [bass.AP(wbuf, 0, [[11264, 128], [1, 5632]]),
               bass.AP(wbuf, 5632, [[11264, 128], [1, 5632]])]

        pf = [pst(f"pf{i}", [128, 512], F32) for i in range(6)]
        pb = [pst(f"pb{i}", [128, 1024], BF16) for i in range(2)]

        pl = Plan()
        Q = pl.op
        M = pl.marks

        def dma_group(eng, stream, pairs, waits=()):
            allw = [(stream, pl.n(stream))] + list(waits)
            first = True
            for dst, src in pairs:
                Q(eng, (lambda e, d=dst, s=src: e.dma_start(out=d, in_=s)),
                  waits=tuple(allw) if first else (), inc=stream, amt=16)
                first = False
            return pl.n(stream)

        def blk64(t_, rows, colhalf):
            r = t_[rows * 64:(rows + 1) * 64, :].rearrange("p (b c) -> p b c", b=16)
            return r[:, :, colhalf * 64:(colhalf + 1) * 64]
        def src3(d_):
            return d_.rearrange("p (b c) -> p b c", b=16)
        M[("lnld", 0)] = dma_group("gpsimd", "dx0",
                                   [(xld[0][:], x_d[0:128, :])])
        Q("vector", lambda v: (v.memset(wq[:], 0.0), v.memset(wv[:], 0.0),
                               v.memset(wk[:], 0.0))[0], inc="V")
        VMSW = pl.n("V")
        CW = dma_group("gpsimd", "dk", [
            (blk64(wq, 0, 0), src3(wq_d[0])), (blk64(wq, 1, 1), src3(wq_d[1])),
            (blk64(wv, 0, 0), src3(wv_d[0])), (blk64(wv, 1, 1), src3(wv_d[1])),
            (blk64(wk, 0, 0), src3(wk_d[0])), (blk64(wk, 1, 1), src3(wk_d[1])),
            (wo[0:64, :], wo_d), (wo[64:128, :], wo_d)],
            waits=(("V", VMSW),))
        CX = dma_group("sync", "dc", [
            (bq[:], bq_d), (bk[:], bk_d), (bv[:], bv_d), (bo[:], bo_d),
            (gss[:], gs_d), (uss[:], us_d), (dss[:], ds_d)])
        Q("vector", lambda v: v.memset(ident[:], 0.0), inc="V")
        VMS = pl.n("V")
        Q("gpsimd", lambda g: g.affine_select(
            out=ident[:], in_=ident[:], compare_op=OP.not_equal, fill=1.0,
            base=0, pattern=[[-1, 128]], channel_multiplier=1),
          waits=(("V", VMS),), inc="Gid")
        GID = pl.n("Gid")
        Q("vector", lambda v: v.tensor_scalar(dsh[:], dss[:], 0.5, None, OP.mult),
          waits=(("dc", CX),), inc="V")
        Q("vector", lambda v: v.tensor_scalar(gsh[:], gss[:], 0.5, None, OP.mult),
          inc="V")
        V_DSH = pl.n("V")

        pl.marks[("stage", "consts")] = (len(pl.ops), dict(pl.cnt))
        xn_v = {}
        gu_mm = {}
        gu_sil = {}
        gu_ev = {}
        dn_mm = {}
        dn_th = {}
        fin_v = {}
        st_after = {}
        pb_guard = [0, 0]
        pb_sguard = [0, 0]
        akb = [0, 0]
        akbS = [0, 0]
        out_dmas = 0
        dn_ev = {}
        dn_sread = {}
        dn_txp = {}
        sev_pair = {}
        emit_down_parts = {}
        s_ready_map = {0: 0, 1: 0}
        emit_down_map = {}
        v_attn_end = 0
        t_xnT_read = 0
        t_hdn_read = 0

        def emit_ln_stats(h, i):
            gi = h * NLN + i
            sl = i % 2
            w = []
            if gi >= 2:
                w.append(("V", xn_v.get(gi - 2, 0)))
                w.append(("T", M.get(("lntx", gi - 2), 0)))
            xs = f"dx{sl}"
            if ("lnld", gi) in M:
                ldx = M.pop(("lnld", gi))
            else:
                ldx = dma_group("gpsimd", xs,
                                [(xld[sl][:], x_d[gi * 128:(gi + 1) * 128, :])],
                                waits=tuple(w))
            w = [(xs, ldx)]
            if gi >= 2:
                w.append(("V", xn_v.get(gi - 2, 0)))
            Q("scalar", (lambda s, s_=sl: s.activation(
                lntr[:], xld[s_][:], ACT.Square, accum_out=stt2[s_][:])),
                waits=tuple(w), inc="S")
            Q("scalar", (lambda s, s_=sl: s.activation(
                lntr[:], xld[s_][:], ACT.Identity,
                accum_out=stt1[s_][:])), inc="S")
            sacc = pl.n("S")

            m_, q2_, ve_, rc_, rs_, tp_ = ln_t
            chain = [
                (lambda v, s_=sl: v.tensor_scalar(m_[:], stt1[s_][:], 1.0 / D,
                                                  None, OP.mult)),
                (lambda v, s_=sl: v.tensor_scalar(q2_[:], stt2[s_][:], 1.0 / D,
                                                  None, OP.mult)),
                (lambda v: v.tensor_tensor(tp_[:], m_[:], m_[:], OP.mult)),
                (lambda v: v.tensor_tensor(ve_[:], q2_[:], tp_[:], OP.subtract)),
                (lambda v: v.tensor_scalar(ve_[:], ve_[:], 1.0, EPS,
                                           OP.mult, OP.add)),
                (lambda v: v.tensor_scalar(rs_[:], ve_[:], -0.5, 1.5,
                                           OP.mult, OP.add)),
            ]
            for it in range(3):
                chain += [
                    (lambda v: v.tensor_tensor(tp_[:], rs_[:], rs_[:], OP.mult)),
                    (lambda v: v.tensor_tensor(tp_[:], tp_[:], ve_[:], OP.mult)),
                    (lambda v: v.tensor_scalar(tp_[:], tp_[:], -0.5, 1.5,
                                               OP.mult, OP.add)),
                    (lambda v: v.tensor_tensor(rs_[:], rs_[:], tp_[:], OP.mult)),
                ]
            chain += [
                (lambda v: v.tensor_tensor(tp_[:], m_[:], rs_[:], OP.mult)),
                (lambda v: v.tensor_scalar(tp_[:], tp_[:], -1.0, None, OP.mult)),
                (lambda v, s_=sl: v.tensor_scalar(xld[s_][:], xld[s_][:],
                                                  rs_[:], tp_[:],
                                                  OP.mult, OP.add)),
            ]
            for cf in chain:
                Q("vector", cf,
                  waits=(("S", sacc), ("V", pl.n("V"))), inc="V",
                  keep_self=True)
            xn_v[gi] = pl.n("V")

        def emit_ln_evac(h, i):
            gi = h * NLN + i
            sl = i % 2
            for bat in range(2):
                pbk = pb[bat]
                wt = [("V", xn_v[gi]), ("Gid", GID), ("V", pb_guard[bat]),
                      ("S", pb_sguard[bat])]

                def xput(t, s_=sl, b=bat, pbk_=pbk):
                    last = None
                    for bb in range(8):
                        blk = b * 8 + bb
                        last = t.transpose(pbk_[:, bb * 128:(bb + 1) * 128],
                                           xld[s_][:, blk * 128:(blk + 1) * 128],
                                           ident[:])
                    return last
                Q("tensor", xput, waits=tuple(wt), inc="T")
                tx = pl.n("T")
                M[("lntx", gi)] = tx

                def xev(v, ii=i, b=bat, pbk_=pbk):
                    last = None
                    for bb in range(8):
                        blk = b * 8 + bb
                        last = v.tensor_copy(
                            xnT[:, blk * HT + ii * 128: blk * HT + (ii + 1) * 128],
                            pbk_[:, bb * 128:(bb + 1) * 128])
                    return last
                wv_ = [("T", tx)]
                if h == 1 and i < NLN - 1:
                    wv_.append(("T", gu_mm[(0, NJ - 1)]))
                else:
                    wv_.append(("T", t_xnT_read))
                Q("vector", xev, waits=tuple(wv_), inc="V")
                pb_guard[bat] = pl.n("V")
            M[("xnTt", h, i)] = pl.n("V")
            pl.marks[("stage", f"lnt{gi}")] = (len(pl.ops), dict(pl.cnt))

        for h in range(2):
            if h == 0:
                emit_ln_stats(0, 0)
                emit_ln_evac(0, 0)
            pl.marks[("stage", f"ln{h}")] = (len(pl.ops), dict(pl.cnt))

            if h == 0:
                Q("scalar", lambda s: s.preload_act_table(ACT.Exp),
                  waits=(("S", pl.n("S")),))

            def emit_gu(j):
                sl = j % 2
                w = []
                if j >= 2:
                    w.append(("T", gu_mm[(h, j - 2)]))
                elif h == 1:
                    w.append(("T", dn_mm[(0, NHH - 1)]))
                gs_ = f"dg{sl}"
                wld = dma_group("gpsimd", gs_,
                                [(gw[sl], gwb_d[j]), (uw[sl], uwb_d[j])],
                                waits=tuple(w))
                pg, pu = pf[0 + sl * 2], pf[1 + sl * 2]
                w = [(gs_, wld), ("V", M[("xnTt", h, NLN - 1)])]
                if j >= 2:
                    w.append(("V", gu_ev[(h, j - 2)]))
                    w.append(("S", gu_sil[(h, j - 2)]))
                elif h == 1:
                    w.append(("V", dn_ev[(0, NHH - 1)]))
                    w.append(("S", dn_th[(0, NHH - 1)]))

                def gumm(t, s_=sl, pg_=pg, pu_=pu):
                    for k in range(NHH):
                        t.matmul(pg_[:], gw[s_][:, k * 128:(k + 1) * 128],
                                 xnT[:, k * HT:(k + 1) * HT],
                                 start=(k == 0), stop=(k == NHH - 1))
                    last = None
                    for k in range(NHH):
                        last = t.matmul(pu_[:], uw[s_][:, k * 128:(k + 1) * 128],
                                        xnT[:, k * HT:(k + 1) * HT],
                                        start=(k == 0), stop=(k == NHH - 1))
                    return last
                Q("tensor", gumm, waits=tuple(w), inc="T")
                gu_mm[(h, j)] = pl.n("T")
                Q("scalar", (lambda s, jj=j, s_=sl, pg_=pg: s.activation(
                    sgt[s_][:], pg_[:], ACT.Tanh, scale=gsh[:, jj:jj + 1])),
                    waits=(("T", gu_mm[(h, j)]), ("V", V_DSH)), inc="S")
                Q("scalar", (lambda s, jj=j, s_=sl, pg_=pg: s.activation(
                    zgt[s_][:], pg_[:], ACT.Identity, scale=gsh[:, jj:jj + 1])),
                    inc="S")
                Q("scalar", (lambda s, jj=j, s_=sl, pu_=pu: s.activation(
                    sut[s_][:], pu_[:], ACT.Identity, scale=uss[:, jj:jj + 1])),
                    inc="S")
                gu_sil[(h, j)] = pl.n("S")
                whd = [("S", gu_sil[(h, j)])]
                if j == 0 and h == 1:
                    whd.append(("T", t_hdn_read))
                Q("vector", (lambda v, s_=sl: v.scalar_tensor_tensor(
                    zgt[s_][:], sgt[s_][:], 1.0, zgt[s_][:], OP.add, OP.mult)),
                    waits=tuple(whd), inc="V")
                Q("vector", (lambda v, jj=j, s_=sl: v.tensor_tensor(
                    hdnT[:, jj * HT:(jj + 1) * HT], zgt[s_][:], sut[s_][:], OP.mult)),
                    inc="V")
                gu_ev[(h, j)] = pl.n("V")

            def emit_attn(c, filler=None, per=8, per_z=0, skip_k=False,
                          next_k=None, lead=0):
                it = attn_stepper(c, skip_k=skip_k, next_k=next_k)
                emitted = 0
                for i, tag in enumerate(it):
                    if filler is None:
                        continue
                    if tag[0] == "P":
                        want = min(per, ((tag[1] + 1 + lead) * per) // NPAIR)
                        filler(want - emitted)
                        emitted = want
                    elif tag[0] == "z":
                        filler(per_z)

            def emit_kphase(hh, c):
                tc = c * CK
                for bat in range(8):
                    b = bat % 2
                    ps_ = pf[4 + b]
                    w = [("V", akb[b]), ("S", akbS[b]), ("V", M[("xnTt", hh, c)]),
                         ("dk", CW)]

                    def kmm(t, bt=bat, ps=ps_, tc_=tc):
                        last = None
                        for q4 in range(4):
                            e = bt * 4 + q4
                            jj = e // 2
                            lhs = wk[:, e * 64:(e + 1) * 64]
                            rhs = xnT[:, jj * HT + tc_: jj * HT + tc_ + CK]
                            t.matmul(ps[0:64, q4 * 128:(q4 + 1) * 128], lhs, rhs,
                                     start=True, stop=True)
                            last = t.matmul(ps[64:128, q4 * 128:(q4 + 1) * 128],
                                            lhs, rhs, start=True, stop=True)
                        return last
                    Q("tensor", kmm, waits=tuple(w), inc="T")
                    tk = pl.n("T")

                    def kev(s, bt=bat, ps=ps_):
                        last = None
                        for q4 in range(4):
                            e = bt * 4 + q4
                            last = s.activation(
                                kdup[:, e * 128:(e + 1) * 128],
                                ps[:, q4 * 128:(q4 + 1) * 128],
                                ACT.Identity, bias=bk[:, e:e + 1])
                        return last
                    Q("scalar", kev,
                      waits=(("T", tk), ("V", M.get(("vs_last",), 0))), inc="S")
                    akbS[b] = pl.n("S")
                M[("kready", hh, c)] = pl.n("S")

            def attn_stepper(c, womm_cb=None, skip_k=False, next_k=None):
                tc = c * CK
                if not skip_k:
                    emit_kphase(h, c)
                kdup_ready_s = M[("kready", h, c)]
                yield ("k", c)

                def do_qse(P):
                    sl = P % 2
                    w = [("V", akb[0]), ("V", akb[1]), ("S", akbS[0]), ("S", akbS[1])]

                    def qvmm(t, Pp=P, tc_=tc):
                        rhs = xnT[:, Pp * HT + tc_: Pp * HT + tc_ + CK]
                        t.matmul(pf[4][0:64, 0:128],
                                 wq[:, Pp * 128: Pp * 128 + 64], rhs,
                                 start=True, stop=True)
                        t.matmul(pf[4][64:128, 0:128],
                                 wq[:, Pp * 128 + 64:(Pp + 1) * 128], rhs,
                                 start=True, stop=True)
                        t.matmul(pf[5][0:64, 0:128],
                                 wv[:, Pp * 128: Pp * 128 + 64], rhs,
                                 start=True, stop=True)
                        return t.matmul(pf[5][64:128, 0:128],
                                        wv[:, Pp * 128 + 64:(Pp + 1) * 128], rhs,
                                        start=True, stop=True)
                    Q("tensor", qvmm, waits=tuple(w), inc="T")
                    tqv = pl.n("T")

                    def qvev(s, Pp=P, s_=sl):
                        s.activation(qp[s_][:], pf[4][:, 0:128], ACT.Identity,
                                     bias=bq[:, Pp:Pp + 1])
                        return s.activation(vpb[s_][:], pf[5][:, 0:128],
                                            ACT.Identity, bias=bv[:, Pp:Pp + 1])
                    Q("scalar", qvev, waits=(("T", tqv),), inc="S")
                    akbS[0] = akbS[1] = pl.n("S")
                    qvdone_s = pl.n("S")

                    w = [("S", qvdone_s), ("S", kdup_ready_s),
                         ("S", M.get(("exps", 0), 0))]

                    def smul(v, s_=sl):
                        qrep = bass.AP(qp[s_], 0, [[CK, 128], [0, 32], [1, CK]])
                        return v.tensor_tensor(
                            sping[:].rearrange("p (r n) -> p r n", r=32),
                            qrep,
                            kdup[:].rearrange("p (r n) -> p r n", r=32),
                            OP.mult)
                    Q("vector", smul, waits=tuple(w), inc="V")
                    vs = pl.n("V")
                    M[("vs_last",)] = vs
                    Q("scalar", (lambda s, s_=sl: s.activation(
                        Et[s_][:], sping[:], ACT.Exp, scale=0.125)),
                        waits=(("V", vs), ("V", M.get(("uctxp", sl), 0))),
                        inc="S")
                    M[("exps", 0)] = pl.n("S")
                    M[("expdone", P % 2)] = pl.n("S")

                def do_zu(P):
                    sl = P % 2

                    def ztree(v, s_=sl):
                        ev = bass.AP(Et[s_], 0, [[NE * CK, 128], [256, 16], [1, 128]])
                        od = bass.AP(Et[s_], 128, [[NE * CK, 128], [256, 16], [1, 128]])
                        v.tensor_tensor(
                            zscr[:, 0:2048].rearrange("p (r n) -> p r n", r=16),
                            ev, od, OP.add)
                        e2 = bass.AP(zscr, 0, [[2048, 128], [256, 8], [1, 128]])
                        o2 = bass.AP(zscr, 128, [[2048, 128], [256, 8], [1, 128]])
                        v.tensor_tensor(
                            zscr[:, 0:1024].rearrange("p (r n) -> p r n", r=8),
                            e2, o2, OP.add)
                        e3 = bass.AP(zscr, 0, [[2048, 128], [256, 4], [1, 128]])
                        o3 = bass.AP(zscr, 128, [[2048, 128], [256, 4], [1, 128]])
                        v.tensor_tensor(
                            zscr[:, 0:512].rearrange("p (r n) -> p r n", r=4),
                            e3, o3, OP.add)
                        e4 = bass.AP(zscr, 0, [[2048, 128], [256, 2], [1, 128]])
                        o4 = bass.AP(zscr, 128, [[2048, 128], [256, 2], [1, 128]])
                        v.tensor_tensor(
                            zscr[:, 0:256].rearrange("p (r n) -> p r n", r=2),
                            e4, o4, OP.add)
                        v.tensor_tensor(Zf[:], zscr[:, 0:128], zscr[:, 128:256],
                                        OP.add)
                        v.reciprocal(Zr[:], Zf[:])
                        return v.tensor_tensor(wP[s_][:], vpb[s_][:], Zr[:], OP.mult)
                    Q("vector", ztree, waits=(("S", M[("expdone", sl)]),), inc="V")
                    wready = pl.n("V")

                    def uctx(v, Pp=P, s_=sl):
                        last = None
                        for mg in range(2):
                            wrep = bass.AP(wP[s_], 0, [[CK, 128], [0, 16], [1, CK]])
                            esl = Et[s_][:, mg * 2048:(mg + 1) * 2048].rearrange(
                                "p (r n) -> p r n", r=16)
                            csl = ctxa[:, mg * 2048:(mg + 1) * 2048].rearrange(
                                "p (r n) -> p r n", r=16)
                            if Pp == 0:
                                last = v.tensor_tensor(csl, wrep, esl, OP.mult)
                            else:
                                ut = utmp[:].rearrange("p (r n) -> p r n", r=16)
                                v.tensor_tensor(ut, wrep, esl, OP.mult)
                                last = v.tensor_tensor(csl, csl, ut, OP.add)
                        return last
                    Q("vector", uctx, waits=(("V", wready),), inc="V")
                    M[("uctxp", sl)] = pl.n("V")

                do_qse(0)
                yield ("P", 0)
                for P in range(1, NPAIR):
                    do_qse(P)
                    if P == NPAIR - 1 and next_k is not None:
                        emit_kphase(*next_k)
                    do_zu(P - 1)
                    yield ("P", P)
                do_zu(NPAIR - 1)
                pctx = pl.n("V")
                yield ("z", c)

                for grp in range(4):
                    b = grp % 2
                    ps_ = pf[4 + b]
                    jps = list(range(grp * 4, grp * 4 + 4))
                    w = [("V", pctx), ("S", akbS[b]), ("V", akb[b])]

                    def womm(t, jj=tuple(jps), ps=ps_):
                        last = None
                        for q, j_ in enumerate(jj):
                            me, mo = 2 * j_, 2 * j_ + 1
                            t.matmul(ps[0:64, q * 128:(q + 1) * 128], wo[:, 0:64],
                                     ctxa[:, me * 128:(me + 1) * 128],
                                     start=True, stop=True)
                            last = t.matmul(ps[64:128, q * 128:(q + 1) * 128],
                                            wo[:, 0:64],
                                            ctxa[:, mo * 128:(mo + 1) * 128],
                                            start=True, stop=True)
                        return last
                    Q("tensor", womm, waits=tuple(w), inc="T")
                    tw = pl.n("T")

                    wsev = [("T", tw)] + [("V", dn_sread.get((h - 1, j_), 0))
                                          for j_ in jps]

                    def sev(s, jj=tuple(jps), tc_=tc, ps=ps_):
                        last = None
                        for q, j_ in enumerate(jj):
                            last = s.activation(
                                s_sb[:, j_ * HT + tc_: j_ * HT + tc_ + CK],
                                ps[:, q * 128:(q + 1) * 128], ACT.Identity,
                                bias=bo[:])
                        return last
                    Q("scalar", sev, waits=tuple(wsev), inc="S")
                    akbS[b] = pl.n("S")
                    akb[b] = max(akb[b], pctx)
                    for j_ in jps:
                        sev_pair[(h, j_)] = pl.n("S")
                    if womm_cb is not None:
                        for j_ in jps:
                            womm_cb(j_)
                s_ready_map[h] = pl.n("S")
                yield ("end", c)

            def emit_xpc_load(hd, hh, eng="sync", pfx="dp"):
                ghh = hd * NHH + hh
                sl = hh % 2
                wx = []
                if ghh >= 2:
                    wx.append(("V", fin_v.get(ghh - 2, 0)))
                ps_str = f"{pfx}{sl}"
                xl = dma_group(eng, ps_str, [(
                    xpc[sl][:].rearrange("p (a b) -> p a b", a=4),
                    bass.AP(x_h, (hd * HT) * D + hh * 128,
                            [[D, 128], [128 * D, 4], [1, 128]]))],
                    waits=tuple(wx))
                M[("dnxl", ghh)] = (ps_str, xl)

            def emit_down_mm(hd, hh, xpc_at_mm=True):
                ghh = hd * NHH + hh
                sl = hh % 2
                w = [("T", gu_mm[(hd, NJ - 1)])]
                if hh >= 2:
                    w.append(("T", dn_mm[(hd, hh - 2)]))
                ds_ = f"dd{sl}"
                dld = dma_group("gpsimd", ds_, [(dwb[sl], dwb_d[hh])],
                                waits=tuple(w))
                if xpc_at_mm:
                    emit_xpc_load(hd, hh)
                pd = pf[hh % 4]
                w = [(ds_, dld), ("V", gu_ev[(hd, NJ - 1)])]
                if hh >= 4:
                    w.append(("S", dn_th[(hd, hh - 4)]))
                    w.append(("V", dn_ev[(hd, hh - 4)]))
                else:
                    w.append(("S", gu_sil[(hd, NJ - 1)]))
                    w.append(("V", gu_ev[(hd, NJ - 1)]))

                def dnmm(t, s_=sl, pd_=pd):
                    last = None
                    for j in range(NJ):
                        last = t.matmul(pd_[:], dwb[s_][:, j * 128:(j + 1) * 128],
                                        hdnT[:, j * HT:(j + 1) * HT],
                                        start=(j == 0), stop=(j == NJ - 1))
                    return last
                Q("tensor", dnmm, waits=tuple(w), inc="T")
                dn_mm[(hd, hh)] = pl.n("T")

            def emit_down_evac(hd, hh, zslot=None, fin=True):
                sl = hh % 2
                pd = pf[hh % 4]
                zs = zslot if zslot is not None else zt[sl][:]
                Q("scalar", (lambda s, k=hh, s_=sl, pd_=pd: s.activation(
                    tht[s_][:], pd_[:], ACT.Tanh, scale=dsh[:, k:k + 1])),
                    waits=(("T", dn_mm[(hd, hh)]), ("V", V_DSH)), inc="S")
                dn_th[(hd, hh)] = pl.n("S")
                wv_ = [("S", dn_th[(hd, hh)])]
                if hd == 1 and zslot is not None:
                    wv_.append(("T", M.get(("lntx", 2 * NLN - 1), 0)))

                def dnev(v, k=hh, s_=sl, pd_=pd, zs_=zs):
                    v.tensor_scalar(zs_, pd_[:], dsh[:, k:k + 1], None, OP.mult)
                    return v.scalar_tensor_tensor(zs_, tht[s_][:], 1.0, zs_,
                                                  OP.add, OP.mult)
                Q("vector", dnev, waits=tuple(wv_), inc="V")
                dn_ev[(hd, hh)] = pl.n("V")
                if fin:
                    emit_down_fin(hd, hh, zslot=zs)

            def emit_down_fin(hd, hh, zslot=None):
                nonlocal out_dmas
                ghh = hd * NHH + hh
                sl = hh % 2
                zs = zslot if zslot is not None else zt[sl][:]
                if ("dnxl", ghh) not in M:
                    emit_xpc_load(hd, hh, eng="gpsimd", pfx="dx")
                ps_str, xl = M[("dnxl", ghh)]

                def dfin(v, k=hh, s_=sl, zs_=zs):
                    return v.tensor_tensor(tht[s_][:], zs_,
                                           s_sb[:, k * HT:(k + 1) * HT], OP.mult)
                Q("vector", dfin, waits=(("V", dn_ev[(hd, hh)]),
                                         ("S", sev_pair.get((hd, hh), 0)),
                                         ("T", dn_txp.get((hd, hh - 2), 0))),
                  inc="V")
                vP_ = pl.n("V")
                dn_sread[(hd, hh)] = vP_
                pbk = pb[hh % 2]
                wt = [("V", vP_), ("Gid", GID), ("V", pb_guard[hh % 2]),
                      ("S", pb_sguard[hh % 2])]

                def pxp(t, s_=sl, pbk_=pbk):
                    last = None
                    for tb in range(4):
                        last = t.transpose(pbk_[:, tb * 128:(tb + 1) * 128],
                                           tht[s_][:, tb * 128:(tb + 1) * 128],
                                           ident[:])
                    return last
                Q("tensor", pxp, waits=tuple(wt), inc="T")
                txp = pl.n("T")
                dn_txp[(hd, hh)] = txp

                def finadd(v, s_=sl, pbk_=pbk, par=(hh % 2)):
                    last = None
                    for tb in range(4):
                        last = v.tensor_tensor(opc[par * 4 + tb][:],
                                               pbk_[:, tb * 128:(tb + 1) * 128],
                                               xpc[s_][:, tb * 128:(tb + 1) * 128],
                                               OP.add)
                    return last
                Q("vector", finadd,
                  waits=(("T", txp), (ps_str, xl),
                         (f"st{hh % 2}", st_after.get(ghh - 2, 0))), inc="V")
                fin_v[ghh] = pl.n("V")
                pb_guard[hh % 2] = fin_v[ghh]
                ss_ = f"st{hh % 2}"
                dma_group("sync", ss_, [(
                    bass.AP(out_h, (hd * HT + tb * 128) * D + hh * 128,
                            [[D, 128], [1, 128]]),
                    opc[(hh % 2) * 4 + tb][:]) for tb in range(4)],
                    waits=(("V", fin_v[ghh]),))
                out_dmas += 4
                st_after[ghh] = pl.n(ss_)

            def emit_down_hh(hd, hh):
                emit_down_mm(hd, hh)
                emit_down_evac(hd, hh)

            emit_down_map[h] = emit_down_hh
            emit_down_parts[h] = (emit_down_mm, emit_down_evac)

            units = []
            if h == 0:
                units += [(lambda ii=i: (emit_ln_stats(0, ii),
                                         emit_ln_evac(0, ii)))
                          for i in (1, 2, 3)]
            units += [(lambda jj=j: emit_gu(jj)) for j in range(NJ)]
            if h == 0:
                for i in range(NLN):
                    units.append(lambda ii=i: emit_ln_stats(1, ii))
                    if i < NLN - 1:
                        units.append(lambda ii=i: emit_ln_evac(1, ii))
            uq = iter(units)
            nu = len(units)

            def drain(n):
                for _ in range(n):
                    u = next(uq, None)
                    if u is not None:
                        u()
            if h == 0:
                for c, per in zip(range(NCH), (11, 13, 13, 17)):
                    emit_attn(c, filler=drain, per=per,
                              lead=(2 if c == 0 else 0))
                drain(nu)
            else:
                g = attn_stepper(0)
                edh = emit_down_map[0]
                dn0 = iter(range(NHH))
                for tag in g:
                    if tag[0] == "k":
                        emit_ln_evac(1, NLN - 1)
                    elif tag[0] == "P":
                        hh = next(dn0, None)
                        if hh is not None:
                            edh(0, hh)
                t_hdn_read = pl.n("T")
                emit_attn(1, filler=drain, per=22)
                emit_attn(2, filler=drain, per=22)
                drain(nu)
                dn1_zsl = ([xld[0][:, q * HT:(q + 1) * HT] for q in range(4)]
                           + [xld[1][:, q * HT:(q + 1) * HT] for q in range(4)]
                           + [sgt[0][:], sgt[1][:], zgt[0][:], zgt[1][:],
                              sut[0][:], sut[1][:], zt[0][:], zt[1][:]])
                dn1 = iter(range(NHH))
                dn1_pend = []

                def dn1_step():
                    if dn1_pend:
                        hhp = dn1_pend.pop(0)
                        emit_down_evac(1, hhp, zslot=dn1_zsl[hhp], fin=False)
                    hh2 = next(dn1, None)
                    if hh2 is not None:
                        emit_down_mm(1, hh2, xpc_at_mm=False)
                        dn1_pend.append(hh2)
                for tag in attn_stepper(3):
                    if tag[0] == "P":
                        dn1_step()
                while True:
                    if dn1_pend:
                        hh2 = dn1_pend.pop(0)
                        emit_down_evac(1, hh2, zslot=dn1_zsl[hh2], fin=False)
                        continue
                    hh2 = next(dn1, None)
                    if hh2 is None:
                        break
                    emit_down_mm(1, hh2, xpc_at_mm=False)
                    dn1_pend.append(hh2)
                for hh2 in range(NHH):
                    emit_down_fin(1, hh2, zslot=dn1_zsl[hh2])
            v_attn_end = pl.n("V")
            t_xnT_read = pl.n("T")

            pl.marks[("stage", f"attn{h}")] = (len(pl.ops), dict(pl.cnt))


            pl.marks[("stage", f"down{h}")] = (len(pl.ops), dict(pl.cnt))

        global LAST_MARKS
        LAST_MARKS = {k2[1]: v[0] for k2, v in pl.marks.items()
                      if isinstance(k2, tuple) and k2[0] == "stage"}
        if stage is not None:
            if isinstance(stage, str) and stage.startswith("op"):
                n_cut = int(stage[2:])
                cc = {}
                for (e_, w_, f_, i_, a_) in pl.ops[:n_cut]:
                    if i_ is not None:
                        cc[i_] = cc.get(i_, 0) + a_
                pl.marks[("stage", stage)] = (n_cut, cc)
            nops, cnts = pl.marks[("stage", stage)]
            pl.ops = pl.ops[:nops]
            dma_owner = {"dc": "sync", "dp0": "sync", "dp1": "sync",
                         "st0": "sync", "st1": "sync",
                         "dk": "gpsimd", "dx0": "gpsimd", "dx1": "gpsimd",
                         "dg0": "gpsimd", "dg1": "gpsimd",
                         "dd0": "gpsimd", "dd1": "gpsimd"}
            for sname, eng in dma_owner.items():
                v = cnts.get(sname, 0)
                if v > 0:
                    pl.ops.append((eng, [(sname, v)], (lambda e: None), None, 1))
            dumps = [("sb_xld0", xld[0]), ("sb_xld1", xld[1]),
                     ("sb_lnt0", ln_t[0]), ("sb_lnt1", ln_t[1]), ("sb_lnt2", ln_t[2]), ("sb_lnt3", ln_t[3]), ("sb_lnt4", ln_t[4]), ("sb_lnt5", ln_t[5]), ("sb_stt1", stt1[0]), ("sb_stt2", stt2[0]),
                     ("sb_wq", wq), ("sb_wk", wk), ("sb_wv", wv), ("sb_xnT", xnT)]
            if not (stage.startswith("ln") or stage.startswith("consts")):
                dumps += [
                     ("sb_kdup", kdup), ("sb_qp0", qp[0]), ("sb_vp0", vpb[0]),
                     ("sb_sping", sping), ("sb_Et", Et), ("sb_Zf", Zf),
                     ("sb_ctxa", ctxa), ("sb_s_sb", s_sb), ("sb_hdnT", hdnT)]
            cw = [("T", cnts.get("T", 0)), ("V", cnts.get("V", 0)),
                  ("S", cnts.get("S", 0))]
            col = 0
            pairs = []
            dump_cols = {}
            for nm, t_ in dumps:
                ncol = t_.shape[1] if not isinstance(t_, bass.AP) else t_.shape[1]
                ncol = min(ncol, 8192)
                dump_cols[nm] = (col, ncol)
                pairs.append((bass.AP(dbg_h, col, [[65536, 128], [1, ncol]]),
                              t_[:, 0:ncol]))
                col += ncol
            global DUMP_COLS
            DUMP_COLS = dump_cols
            for idx, (d_, s_) in enumerate(pairs):
                pl.ops.append(("gpsimd", cw if idx == 0 else [], (
                    lambda e, dd=d_, ss=s_: e.dma_start(out=dd, in_=ss, allow_slow_non_contiguous=True)), "st0", 16))
            pl.ops.append(("gpsimd", [("st0", cnts.get("st0", 0) + 16 * len(pairs))],
                           (lambda e: None), None, 1))
        else:
            Q("sync", lambda e: None,
              waits=(("st0", pl.n("st0")), ("st1", pl.n("st1"))))

        block = ctx.enter_context(nc.Block())

        def runner(engname):
            def body(eng):
                for (e, waits, emit, inc, amt) in pl.ops:
                    if e != engname:
                        continue
                    for (s, v) in waits:
                        eng.wait_ge(semh[s], v)
                    inst = emit(eng)
                    if inc is not None:
                        assert inst is not None, f"{engname} op with inc returned None"
                        inst.then_inc(semh[inc], amt)
            return body

        block.sync(runner("sync"))
        block.gpsimd(runner("gpsimd"))
        block.tensor(runner("tensor"))
        block.vector(runner("vector"))
        block.scalar(runner("scalar"))

    return nc


LAST_MARKS = {}
DUMP_COLS = {}
_NC_CACHE = None


def _get_nc():
    global _NC_CACHE
    if _NC_CACHE is None:
        _NC_CACHE = build_program()
    return _NC_CACHE


def kernel(L0_attn_x, L0_attn_Wqkv, L0_attn_bqkv, L0_attn_Wout, L0_attn_bout,
           L0_ffn_gate_w, L0_ffn_up_w, L0_ffn_down_w, L0_ffn_gate_s, L0_ffn_up_s,
           L0_ffn_down_s):
    asc = np.ascontiguousarray
    f32 = np.float32
    x = asc(np.asarray(L0_attn_x, f32).reshape(TOK * NCORES, D))
    Wqkv = np.asarray(L0_attn_Wqkv, f32)
    bqkv = np.asarray(L0_attn_bqkv, f32)
    Wout = asc(np.asarray(L0_attn_Wout, f32))
    bout = np.asarray(L0_attn_bout, f32)
    gate_w = np.asarray(L0_ffn_gate_w, f32)
    up_w = np.asarray(L0_ffn_up_w, f32)
    down_w = np.asarray(L0_ffn_down_w, f32)
    gate_s = np.asarray(L0_ffn_gate_s, f32)
    up_s = np.asarray(L0_ffn_up_s, f32)
    down_s = np.asarray(L0_ffn_down_s, f32)

    gate_blk = asc(gate_w.T.reshape(NHH, 128, NJ, 128).transpose(2, 1, 0, 3)
                   ).reshape(NJ, 128, NHH * 128)
    up_blk = asc(up_w.T.reshape(NHH, 128, NJ, 128).transpose(2, 1, 0, 3)
                 ).reshape(NJ, 128, NHH * 128)
    down_blk = asc(down_w.T.reshape(NJ, 128, NHH, 128).transpose(2, 1, 0, 3)
                   ).reshape(NHH, 128, NJ * 128)
    wq = asc(np.stack([Wqkv[0::2, :, 0:64].transpose(1, 0, 2).reshape(64, NPAIR * 64),
                       Wqkv[1::2, :, 0:64].transpose(1, 0, 2).reshape(64, NPAIR * 64)]))
    wv = asc(np.stack([Wqkv[0::2, :, 128:192].transpose(1, 0, 2).reshape(64, NPAIR * 64),
                       Wqkv[1::2, :, 128:192].transpose(1, 0, 2).reshape(64, NPAIR * 64)]))
    wk = asc(np.stack([Wqkv[0::2, :, 64:128].transpose(1, 0, 2).reshape(64, NPAIR * 64),
                       Wqkv[1::2, :, 64:128].transpose(1, 0, 2).reshape(64, NPAIR * 64)]))
    bq = asc(bqkv[:, 0:64].reshape(NPAIR, 2, 64).transpose(1, 2, 0).reshape(128, NPAIR))
    bkh = asc(np.concatenate([bqkv[:, 64:128].T, bqkv[:, 64:128].T], axis=0))
    bv = asc(bqkv[:, 128:192].reshape(NPAIR, 2, 64).transpose(1, 2, 0).reshape(128, NPAIR))
    bo = asc(np.concatenate([bout, bout])[:, None])
    gs = asc(gate_s.reshape(NJ, 128).T)
    us = asc(up_s.reshape(NJ, 128).T)
    ds = asc(down_s.reshape(NHH, 128).T)

    nc = _get_nc()
    shared = {
        "gate_blk": gate_blk, "up_blk": up_blk, "down_blk": down_blk,
        "wq": wq, "wk": wk, "wv": wv, "wo": Wout,
        "bq": bq, "bk": bkh, "bv": bv, "bo": bo,
        "gs": gs, "us": us, "ds": ds,
    }
    in_maps = []
    for c in range(NCORES):
        m = dict(shared)
        m["x"] = asc(x[c * TOK:(c + 1) * TOK])
        in_maps.append(m)
    res = run_bass_kernel_spmd(nc, in_maps, core_ids=list(range(NCORES)))
    out = np.concatenate([np.asarray(r["out"]) for r in res.results], axis=0)
    return out.reshape(4, 2048, 2048).astype(np.float32)


if __name__ == "__main__":
    print("building program...")
    nc = build_program()
    print("built ok")
